# version 29
# baseline (speedup 1.0000x reference)
import contextlib

import numpy as np

import concourse.bass as bass
import concourse.mybir as mybir
from concourse.bass_utils import run_bass_kernel_spmd

F32 = mybir.dt.float32
I32 = mybir.dt.int32
AF = mybir.ActivationFunctionType
ALU = mybir.AluOpType
AX = mybir.AxisListType

T = 2048
D = 1024
DKV = 256
NKV = 4
DI = 2816
NKC = D // 128
NTT = T // 128
NQC = T // 512
NHC = DI // 128
HBS = [(0, 512), (512, 512), (1024, 512), (1536, 512), (2048, 512), (2560, 256)]
R6 = float(np.float32(1.0) / np.float32(6.0))

SEMS = ["qC", "qX0", "qX1", "qW0", "qW1", "qO0", "qO1", "qG", "qG1", "qGW", "qGW1",
        "t2v", "t2s", "t2g", "v2t", "v2s", "v2q", "v2g", "vv",
        "s2t", "s2v", "s2q", "s2g", "g2t"]


class EV:

    def __init__(self, name, eng, sems, log=None):
        self.name = name
        self.eng = eng
        self.sems = sems
        self.c = {k: 0 for k in SEMS}
        self.log = log
        self.label = ""

    def mark(self, label):
        self.label = label

    def wait(self, on_eng, sem, val=None):
        v = self.c[sem] if val is None else val
        if self.name == on_eng:
            if self.log is not None:
                self.log.append(("wait", sem, v, self.label))
            if self.eng is not None:
                self.eng.wait_ge(self.sems[sem], v)

    def _marker(self, e):
        if self.name == "tensor":
            return e.nop()
        if self.name in ("vector", "gpsimd"):
            return e.engine_nop()
        if self.name == "scalar":
            return e.drain()
        return e.nop()

    def do(self, on_eng, fn, sem=None, n=1, also=()):
        if sem is not None:
            self.c[sem] += n
        if self.name == on_eng:
            if self.log is not None and sem is not None:
                self.log.append(("inc", sem, self.c[sem], self.label))
            if self.log is not None and sem is None:
                self.log.append(("inc", None, None, self.label))
            if self.eng is not None:
                ins = fn(self.eng)
                if sem is not None:
                    ins.then_inc(self.sems[sem], n)
        for s2, n2 in also:
            self.c[s2] += n2
            if self.name == on_eng:
                if self.log is not None:
                    self.log.append(("inc", s2, self.c[s2], self.label))
                if self.eng is not None:
                    self._marker(self.eng).then_inc(self.sems[s2], n2)

    def drain(self, on_eng, sem=None, n=1, also=()):
        self.do(on_eng, lambda e: e.drain(), sem=sem, n=n, also=also)


def _check_deadlock(logs):
    pos = {e: 0 for e in logs}
    cnt = {s: 0 for s in SEMS}
    progressed = True
    while progressed:
        progressed = False
        for e, ops in logs.items():
            while pos[e] < len(ops):
                kind, sem, val, label = ops[pos[e]]
                if kind == "wait" and sem is not None and cnt[sem] < val:
                    break
                if kind == "inc" and sem is not None:
                    cnt[sem] = max(cnt[sem], val)
                pos[e] += 1
                progressed = True
    stuck = {e: (pos[e], len(ops), logs[e][pos[e]]) for e, ops in logs.items()
             if pos[e] < len(ops)}
    if stuck:
        raise RuntimeError(f"schedule deadlock: {stuck}")


def build_nc(tap=None, sim_check=True):
    nc = bass.Bass("TRN2", target_bir_lowering=False)

    x = nc.dram_tensor("x", [T, D], F32, kind="ExternalInput").ap()
    Wq = nc.dram_tensor("Wq", [D, D], F32, kind="ExternalInput").ap()
    bq = nc.dram_tensor("bq", [D], F32, kind="ExternalInput").ap()
    Wk = nc.dram_tensor("Wk", [D, DKV], F32, kind="ExternalInput").ap()
    bk = nc.dram_tensor("bk", [DKV], F32, kind="ExternalInput").ap()
    Wv = nc.dram_tensor("Wv", [D, DKV], F32, kind="ExternalInput").ap()
    bv = nc.dram_tensor("bv", [DKV], F32, kind="ExternalInput").ap()
    lng = nc.dram_tensor("lng", [D], F32, kind="ExternalInput").ap()
    lnb = nc.dram_tensor("lnb", [D], F32, kind="ExternalInput").ap()
    Wo = nc.dram_tensor("Wo", [D, D], F32, kind="ExternalInput").ap()
    bo = nc.dram_tensor("bo", [D], F32, kind="ExternalInput").ap()
    wg = nc.dram_tensor("wg", [DI, D], F32, kind="ExternalInput").ap()
    wu = nc.dram_tensor("wu", [DI, D], F32, kind="ExternalInput").ap()
    wd = nc.dram_tensor("wd", [D, DI], F32, kind="ExternalInput").ap()
    y = nc.dram_tensor("y", [T, D], F32, kind="ExternalOutput").ap()
    dbg = None
    if tap is not None:
        dbg = nc.dram_tensor("dbg", [128, 24576], F32, kind="ExternalOutput").ap()

    wgq_s = nc.dram_tensor("wgq_s", [len(HBS), NKC, 128, 512], F32, kind="Internal").ap()
    wuq_s = nc.dram_tensor("wuq_s", [len(HBS), NKC, 128, 512], F32, kind="Internal").ap()
    wdq_s = nc.dram_tensor("wdq_s", [NKC, NHC, 128, 128], F32, kind="Internal").ap()

    ident_d = nc.inline_tensor(np.eye(128, dtype=np.float32), name="ident_c").ap()
    cvals = np.zeros((128, 8), dtype=np.float32)
    cvals[:, 0] = 1e-5
    cvals[:, 1] = -20.0
    cvals[:, 2] = 0.125
    cst_d = nc.inline_tensor(cvals, name="cst_c").ap()

    cx = contextlib.ExitStack()
    sb = lambda nm, shape: cx.enter_context(nc.sbuf_tensor(nm, shape, F32))

    ident = sb("ident", [128, 128])
    cst = sb("cst", [128, 8])
    biasf = sb("biasf", [128, 40])
    stats = sb("stats", [128, 96])
    qstat = sb("qstat", [128, 512])
    xbuf = sb("xbuf", [128, 2 * 1024])
    kfm = sb("kfm", [128, 2 * T])
    vT = sb("vT", [128, NKV * 16 * 65])
    qfm = sb("qfm", [128, NKC * 512])
    xnfm = sb("xnfm", [128, NKC * 512])
    attn = sb("attn", [128, 4 * 1024])
    sfm = sb("sfm", [128, NKC * 512])
    probs = sb("probs", [128, 4 * 512])
    ffnw = sb("ffnw", [128, 2 * 4096])
    hqfm = sb("hqfm", [128, NHC * 512])
    vfm = hqfm[:, 0:2 * T]
    otmp = sb("otmp", [128, 2 * 1024])
    ftmp = sb("ftmp", [128, 2 * 512])

    wbuf = ffnw[:, 0:4096]
    wdbuf = ffnw[:, 0:NHC * 128]
    wstage = ffnw[:, 4096:8192]

    ps = [cx.enter_context(nc.psum_tensor(f"p{i}", [128, 512], F32)) for i in range(8)]
    sems = {nm: cx.enter_context(nc.semaphore(nm)) for nm in SEMS}

    def schedule(E: EV):
        qbm = qstat[:, 0:64]
        qrs = qstat[:, 64:128]

        E.mark("P0")
        E.do("sync", lambda e: e.dma_start(out=ident[:, :], in_=ident_d[:, :]), sem="qC", n=16)
        E.do("sync", lambda e: e.dma_start(out=cst[:, :], in_=cst_d[:, :]), sem="qC", n=16)
        E.do("sync", lambda e: e.dma_start(out=biasf[:, 0:8], in_=bq.rearrange("(o p) -> p o", p=128), allow_slow_non_contiguous=True), sem="qC", n=16)
        E.do("sync", lambda e: e.dma_start(out=biasf[:, 8:16], in_=bo.rearrange("(o p) -> p o", p=128), allow_slow_non_contiguous=True), sem="qC", n=16)
        E.do("sync", lambda e: e.dma_start(out=biasf[:, 16:18], in_=bk.rearrange("(o p) -> p o", p=128), allow_slow_non_contiguous=True), sem="qC", n=16)
        E.do("sync", lambda e: e.dma_start(out=biasf[:, 18:20], in_=bv.rearrange("(o p) -> p o", p=128), allow_slow_non_contiguous=True), sem="qC", n=16)
        E.do("sync", lambda e: e.dma_start(out=biasf[:, 20:28], in_=lng.rearrange("(o p) -> p o", p=128), allow_slow_non_contiguous=True), sem="qC", n=16)
        E.do("sync", lambda e: e.dma_start(out=biasf[:, 28:36], in_=lnb.rearrange("(o p) -> p o", p=128), allow_slow_non_contiguous=True), sem="qC", n=16)
        c0 = E.c["qC"]
        E.wait("vector", "qC", c0)
        E.do("vector", lambda e: e.tensor_single_scalar(biasf[:, 0:8], biasf[:, 0:8], 0.125, ALU.mult))
        for i in range(NKV * 16):
            col = i * 65 + 64
            E.do("vector", lambda e, col=col: e.memset(vT[:, col:col + 1], 1.0))
        E.drain("vector", sem="v2s", also=(("v2t", 1),))
        E.wait("scalar", "qC", c0)
        E.wait("scalar", "v2s")
        E.wait("tensor", "qC", c0)
        E.wait("tensor", "v2t")

        def load_x(tt, b):
            xa = xbuf[:, b * 1024:(b + 1) * 1024]
            qs = "qX0" if b == 0 else "qX1"
            E.wait("sync", "v2q")
            E.wait("sync", "s2q")
            E.do("sync", lambda e, tt=tt, xa=xa: e.dma_start(
                out=xa, in_=x[tt * 128:(tt + 1) * 128, :]), sem=qs, n=16)
            return xa, qs

        E.mark("P1")
        for tt in range(NTT):
            xa, qs = load_x(tt, tt % 2)
            E.wait("vector", qs)
            E.do("vector", lambda e, tt=tt, xa=xa: e.reduce_sum(
                stats[:, 32 + tt:33 + tt], xa, axis=AX.X), sem="v2q")
            E.wait("scalar", qs)
            E.do("scalar", lambda e, tt=tt, xa=xa: e.activation(
                otmp[:, 0:1024], xa, AF.Square, accum_out=stats[:, 48 + tt:49 + tt]), sem="s2q")
        E.drain("scalar", sem="s2v")
        E.wait("vector", "s2v")
        E.drain("vector")
        E.do("vector", lambda e: e.tensor_single_scalar(stats[:, 64:80], stats[:, 32:48], 1.0 / 1024.0, ALU.mult))
        E.drain("vector")
        E.do("vector", lambda e: e.tensor_single_scalar(stats[:, 32:48], stats[:, 48:64], 1.0 / 1024.0, ALU.mult))
        E.drain("vector")
        E.do("vector", lambda e: e.tensor_tensor(stats[:, 80:96], stats[:, 64:80], stats[:, 64:80], ALU.mult))
        E.drain("vector")
        E.do("vector", lambda e: e.tensor_tensor(stats[:, 48:64], stats[:, 32:48], stats[:, 80:96], ALU.subtract))
        E.drain("vector", sem="v2s")
        E.wait("scalar", "v2s")
        E.do("scalar", lambda e: e.activation(stats[:, 32:48], stats[:, 48:64], AF.Sqrt, bias=cst[:, 0:1]))
        E.drain("scalar", sem="s2v")
        E.wait("vector", "s2v")
        E.do("vector", lambda e: e.reciprocal(stats[:, 0:16], stats[:, 32:48]))
        E.drain("vector")
        E.do("vector", lambda e: e.scalar_tensor_tensor(stats[:, 16:32], stats[:, 64:80], -1.0, stats[:, 0:16], ALU.mult, ALU.mult))
        E.drain("vector", sem="v2s", also=(("v2t", 1), ("v2q", 1)))
        E.wait("scalar", "v2s")
        if tap == "stats":
            E.wait("sync", "v2q")
            E.do("sync", lambda e: e.dma_start(out=dbg[:, 0:96], in_=stats[:, 0:96]), sem="qO0", n=16)
            return

        def xn_tile(tt, xa, dst):
            E.do("scalar", lambda e, tt=tt, xa=xa, dst=dst: e.activation(
                dst, xa, AF.Identity, scale=stats[:, tt:tt + 1],
                bias=stats[:, 16 + tt:17 + tt]), sem="s2q")

        def quant_group(items):
            n = len(items)
            Fs = [it[0].free_size() for it in items]
            nbs = [f // 16 for f in Fs]
            assert sum(nbs) <= 256, nbs
            bmo = [0] * n
            off = 0
            for i, nb in enumerate(nbs):
                bmo[i] = off
                off += nb
            bms = [qstat[:, 2 * bmo[i]:2 * bmo[i] + nbs[i]] for i in range(n)]
            rss = [qstat[:, 256 + 2 * bmo[i]:256 + 2 * bmo[i] + nbs[i]] for i in range(n)]
            bms = [qstat[:, bmo[i]:bmo[i] + nbs[i]] for i in range(n)]
            rss = [qstat[:, 256 + bmo[i]:256 + bmo[i] + nbs[i]] for i in range(n)]
            E.wait("scalar", "v2s")
            E.wait("scalar", "v2t")
            for (bi, ba, _s), F in zip(items, Fs):
                E.do("scalar", lambda e, bi=bi, ba=ba: e.activation(ba, bi, AF.Abs))
            E.drain("scalar", sem="s2v")
            E.wait("vector", "s2v")
            a3s = [ba.rearrange("p (b k) -> p b k", k=16) for (_b, ba, _s) in items]
            for i in range(n):
                E.do("vector", lambda e, i=i: e.reduce_max(bms[i], a3s[i], axis=AX.X))
            E.drain("vector")
            for i in range(n):
                E.do("vector", lambda e, i=i: e.tensor_scalar(
                    bms[i], bms[i], R6, 1e-12, ALU.mult, ALU.max))
            E.drain("vector")
            for i in range(n):
                E.do("vector", lambda e, i=i: e.reciprocal(rss[i], bms[i]))
            E.drain("vector")
            rs_b = [rss[i].unsqueeze(2).broadcast_to([128, nbs[i], 16]) for i in range(n)]
            bm_b = [bms[i].unsqueeze(2).broadcast_to([128, nbs[i], 16]) for i in range(n)]
            for i in range(n):
                E.do("vector", lambda e, i=i: e.tensor_tensor(a3s[i], a3s[i], rs_b[i], ALU.mult))
            E.drain("vector")
            for j, thr in enumerate((0.25, 0.75, 1.25, 1.75, 2.5, 3.5, 5.0)):
                for i, (bi, ba, scr) in enumerate(items):
                    E.do("vector", lambda e, i=i, j=j, thr=thr, ba=ba, scr=scr: e.tensor_single_scalar(
                        scr[j], ba, thr, ALU.is_gt))
            E.drain("vector")
            for a, bdx in ((0, 1), (2, 3), (4, 5)):
                for i, (bi, ba, scr) in enumerate(items):
                    E.do("vector", lambda e, scr=scr, a=a, bdx=bdx: e.tensor_tensor(
                        scr[a], scr[a], scr[bdx], ALU.add))
            E.drain("vector")
            for i, (bi, ba, scr) in enumerate(items):
                E.do("vector", lambda e, scr=scr: e.tensor_tensor(scr[0], scr[0], scr[2], ALU.add))
            for i, (bi, ba, scr) in enumerate(items):
                E.do("vector", lambda e, scr=scr: e.scalar_tensor_tensor(scr[4], scr[6], 2.0, scr[4], ALU.mult, ALU.add))
            for i, (bi, ba, scr) in enumerate(items):
                E.do("vector", lambda e, bi=bi, scr=scr: e.tensor_single_scalar(
                    scr[1].bitcast(I32), bi.bitcast(I32), -0x80000000, ALU.bitwise_and))
            E.drain("vector")
            for i, (bi, ba, scr) in enumerate(items):
                E.do("vector", lambda e, scr=scr: e.scalar_tensor_tensor(scr[0], scr[0], 0.5, scr[4], ALU.mult, ALU.add))
            E.drain("vector")
            for i, (bi, ba, scr) in enumerate(items):
                E.do("vector", lambda e, scr=scr: e.tensor_tensor(
                    scr[0].bitcast(I32), scr[0].bitcast(I32), scr[1].bitcast(I32), ALU.bitwise_or))
            E.drain("vector")
            for i, (bi, ba, scr) in enumerate(items):
                o3 = items[i][0].rearrange("p (b k) -> p b k", k=16)
                c3 = items[i][2][0].rearrange("p (b k) -> p b k", k=16)
                E.do("vector", lambda e, i=i, o3=o3, c3=c3: e.tensor_tensor(o3, c3, bm_b[i], ALU.mult),
                     sem=("v2s" if i == n - 1 else None))

        tbank_evac = {0: 0, 1: 0}

        def transpose_tiles(srcs, dst_fn, rows=128, extra_last=()):
            for i in range(len(srcs)):
                bk = i % 2
                pb = ps[bk]
                E.wait("tensor", "v2t", tbank_evac[bk])
                b = srcs[i].base_partition()
                E.do("tensor", lambda e, i=i, pb=pb, b=b: e.transpose(
                    pb[:, 0:rows], srcs[i], ident[b:b + rows, b:b + rows]), sem="t2v")
                E.wait("vector", "t2v")
                E.do("vector", lambda e, i=i, pb=pb: dst_fn(i, pb[:, 0:rows]),
                     sem="v2t")
                tbank_evac[bk] = E.c["v2t"]
            for s2, n2 in extra_last:
                E.drain("vector", sem=s2, n=n2)

        def transpose_tiles4(srcs, dst4_fn, rows=128, nb=4, extra_last=()):
            nbat = (len(srcs) + nb - 1) // nb
            for b in range(nbat):
                bk = b % 2
                pb = ps[bk]
                batch = srcs[b * nb:(b + 1) * nb]
                E.wait("tensor", "v2t", tbank_evac[bk])
                for j, s in enumerate(batch):
                    bp = s.base_partition()
                    E.do("tensor", lambda e, s=s, pb=pb, j=j, bp=bp, rows=rows: e.transpose(
                        pb[:, j * rows:(j + 1) * rows], s, ident[bp:bp + rows, bp:bp + rows]),
                        sem=("t2v" if j == len(batch) - 1 else None))
                E.wait("vector", "t2v")
                E.do("vector", lambda e, b=b, pb=pb: dst4_fn(b, pb), sem="v2t")
                tbank_evac[bk] = E.c["v2t"]
            for s2, n2 in extra_last:
                E.drain("vector", sem=s2, n=n2)

        E.mark("P2")
        wkv = ffnw[:, 0:2048]
        wvv = ffnw[:, 2048:4096]
        E.wait("sync", "qGW")
        E.wait("sync", "qGW1")
        E.do("sync", lambda e: e.dma_start(out=wkv.rearrange("p (k o) -> p k o", o=DKV),
                                           in_=Wk.rearrange("(k p) o -> p k o", p=128)), sem="qC", n=16)
        E.do("sync", lambda e: e.dma_start(out=wvv.rearrange("p (k o) -> p k o", o=DKV),
                                           in_=Wv.rearrange("(k p) o -> p k o", p=128)), sem="qC", n=16)
        c_wkv = E.c["qC"]
        for tc in range(NQC):
            E.mark(f"P2.t{tc}")
            for ttl in range(4):
                tt = tc * 4 + ttl
                xa, qs = load_x(tt, tt % 2)
                E.wait("scalar", qs)
                E.wait("scalar", "t2s")
                E.wait("scalar", "qO0" if ttl % 2 == 0 else "qO1")
                xd = otmp[:, (ttl % 2) * 1024:(ttl % 2) * 1024 + 1024]
                xn_tile(tt, xa, xd)
                E.drain("scalar", sem="s2t")
                E.wait("tensor", "s2t")
                if tc > 0 and ttl == 0:
                    E.wait("tensor", "s2v")
                xnfm3 = xnfm.rearrange("p (k c) -> p k c", c=512)
                transpose_tiles4(
                    [xd[:, kc * 128:(kc + 1) * 128] for kc in range(NKC)],
                    lambda b, pb, ttl=ttl, xnfm3=xnfm3: E.eng.tensor_copy(
                        xnfm3[:, b * 4:b * 4 + 4, ttl * 128:ttl * 128 + 128],
                        pb[:, 0:512].rearrange("p (k c) -> p k c", c=128)))
                E.drain("tensor", sem="t2s")
            E.drain("vector", sem="v2t")
            E.wait("tensor", "v2t")
            E.wait("tensor", "qC", c_wkv)
            for wsb, bcol, dst in ((wkv, 16, kfm), (wvv, 18, vfm)):
                for oc in range(2):
                    pb = ps[2 + oc]
                    E.wait("tensor", "s2v")
                    for kc in range(NKC):
                        E.do("tensor", lambda e, pb=pb, wsb=wsb, oc=oc, kc=kc: e.matmul(
                            pb[:, 0:512], wsb[:, kc * 256 + oc * 128: kc * 256 + oc * 128 + 128],
                            xnfm[:, kc * 512:(kc + 1) * 512],
                            start=(kc == 0), stop=(kc == NKC - 1)),
                            sem=("t2s" if kc == NKC - 1 else None))
                    E.wait("scalar", "t2s")
                    E.do("scalar", lambda e, pb=pb, dst=dst, bcol=bcol, oc=oc, tc=tc: e.activation(
                        dst[:, oc * T + tc * 512: oc * T + tc * 512 + 512], pb[:, 0:512],
                        AF.Identity, bias=biasf[:, bcol + oc:bcol + oc + 1]), sem="s2v")
        E.drain("scalar", sem="s2t", also=(("s2v", 1), ("s2q", 1)))
        if tap == "kv":
            E.wait("sync", "s2v")
            E.do("sync", lambda e: e.dma_start(out=dbg[:, 0:4096], in_=kfm[:, :]), sem="qO0", n=16)
            E.do("sync", lambda e: e.dma_start(out=dbg[:, 4096:8192], in_=vfm[:, :]), sem="qO1", n=16)
            return

        E.mark("P2b")
        E.wait("tensor", "s2v")
        vsrc = []
        for hk in range(NKV):
            for kc in range(16):
                poff = (hk % 2) * 64
                ch = hk // 2
                vsrc.append(vfm[poff:poff + 64, ch * T + kc * 128: ch * T + kc * 128 + 128])
        vT3 = vT.rearrange("p (s c) -> p s c", c=65)
        transpose_tiles4(
            vsrc,
            lambda b, pb, vT3=vT3: E.eng.tensor_copy(
                vT3[:, b * 4:b * 4 + 4, 0:64],
                pb[:, 0:256].rearrange("p (s c) -> p s c", c=64)),
            rows=64)
        E.drain("tensor", sem="t2g")
        E.drain("vector", sem="v2t")
        if tap == "vt":
            E.drain("vector", sem="v2q")
            E.wait("sync", "v2q")
            E.do("sync", lambda e: e.dma_start(out=dbg[:, 0:NKV * 16 * 65], in_=vT[:, :]), sem="qO0", n=16)
            return

        E.mark("PW")
        pw_abs = probs[:, 0:2048]
        pw_slots2 = ([attn[:, j * 1024:(j + 1) * 1024] for j in range(4)]
                     + [sfm[:, j * 1024:(j + 1) * 1024] for j in range(4)]
                     + [qfm[:, j * 1024:(j + 1) * 1024] for j in range(4)]
                     + [ftmp[:, 0:1024], hqfm[:, 10240:11264]])
        RING = [hqfm[:, i * 2048:(i + 1) * 2048] for i in (2, 3, 4, 0, 1)]
        wstage_ab = (ffnw[:, 0:4096], ffnw[:, 4096:8192])
        fl_sem = ("qGW", "qGW1")
        fl_cnt = {0: 0, 1: 0}

        def pw_items(subs, width):
            items = []
            for i, b in enumerate(subs):
                ab = pw_abs[:, i * 1024:i * 1024 + width]
                scr = [pw_slots2[i * 7 + j][:, 0:width] for j in range(7)]
                items.append((b, ab, scr))
            return items

        units = []
        for w_src, w_dst in ((wg, wgq_s), (wu, wuq_s)):
            for hb, (h0, hw) in enumerate(HBS):
                nhcl = hw // 128
                for pp in range(nhcl // 2):
                    units.append((w_src, w_dst, hb, h0, hw, 2 * pp))
        NU = len(units)
        qg_cnt = [0] * NU
        qg1_cnt = [0] * NU

        def pw_load(k):
            w_src = units[k][0]
            h0, hw, hcl0 = units[k][3], units[k][4], units[k][5]
            buf = RING[k % 5]
            for ci in range(2):
                hc = h0 // 128 + hcl0 + ci
                w_in = buf[:, ci * 1024:ci * 1024 + 1024]
                E.do("gpsimd", lambda e, w_in=w_in, hc=hc, w_src=w_src: e.dma_start(
                    out=w_in, in_=w_src[hc * 128:(hc + 1) * 128, :]),
                    sem=("qG" if ci == 0 else "qG1"), n=16)
            qg_cnt[k] = E.c["qG"]
            qg1_cnt[k] = E.c["qG1"]

        for k in range(3):
            pw_load(k)
        E.wait("gpsimd", "t2g")
        pw_load(3)
        wsa = 0
        for k in range(NU):
            w_src, w_dst, hb, h0, hw, hcl0 = units[k]
            buf = RING[k % 5]
            E.wait("scalar", "qG")
            E.wait("scalar", "qG1")
            E.wait("vector", "qG")
            E.wait("vector", "qG1")
            quant_group(pw_items([buf[:, 0:1024], buf[:, 1024:2048]], 1024))
            E.wait("tensor", "v2s")
            wst = wstage_ab[wsa]
            if hcl0 == 0:
                E.wait("vector", fl_sem[wsa])
            srcs = []
            for ci in range(2):
                for kc in range(NKC):
                    srcs.append(buf[:, ci * 1024 + kc * 128: ci * 1024 + (kc + 1) * 128])
            wst3 = wst[:, 0:NKC * hw].rearrange("p (k c) -> p k c", c=hw)

            def pw_evac(b, pb, wst3=wst3, hcl0=hcl0):
                ci = b // 2
                kc0 = (b % 2) * 4
                hcl = hcl0 + ci
                return E.eng.tensor_copy(
                    wst3[:, kc0:kc0 + 4, hcl * 128:hcl * 128 + 128],
                    pb[:, 0:512].rearrange("p (k c) -> p k c", c=128))
            transpose_tiles4(srcs, pw_evac, extra_last=(("v2g", 1),))
            E.drain("tensor", sem="t2g")
            if k + 4 < NU:
                E.wait("gpsimd", "t2g")
                pw_load(k + 4)
            if hcl0 + 2 == hw // 128:
                E.wait("gpsimd", "v2g")
                src = wst[:, 0:NKC * hw].rearrange("p (k c) -> p k c", c=hw)
                dst = w_dst[hb, :, :, 0:hw].rearrange("k p c -> p k c")
                E.do("gpsimd", lambda e, dst=dst, src=src: e.dma_start(
                    out=dst, in_=src), sem=fl_sem[wsa], n=16)
                fl_cnt[wsa] = E.c[fl_sem[wsa]]
                wsa ^= 1
        wd_stag = (hqfm[:, 0:2816], hqfm[:, 4096:6912])
        wd_qg = [0] * NKC
        wd_qg1 = [0] * NKC

        def wd_load(oc):
            stag = wd_stag[oc % 2]
            E.do("gpsimd", lambda e, oc=oc, stag=stag: e.dma_start(
                out=stag[:, 0:1408], in_=wd[oc * 128:(oc + 1) * 128, 0:1408]),
                sem="qG", n=16)
            E.do("gpsimd", lambda e, oc=oc, stag=stag: e.dma_start(
                out=stag[:, 1408:2816], in_=wd[oc * 128:(oc + 1) * 128, 1408:2816]),
                sem="qG1", n=16)
            wd_qg[oc] = E.c["qG"]
            wd_qg1[oc] = E.c["qG1"]

        E.wait("gpsimd", "t2g")
        wd_load(0)
        wd_load(1)
        for oc in range(NKC):
            stag = wd_stag[oc % 2]
            E.wait("scalar", "qG")
            E.wait("scalar", "qG1")
            E.wait("vector", "qG")
            E.wait("vector", "qG1")
            for grp in range(2):
                subs = [stag[:, (grp * 2 + i) * 704:(grp * 2 + i + 1) * 704] for i in range(2)]
                quant_group(pw_items(subs, 704))
            E.wait("tensor", "v2s")
            wst = wstage_ab[wsa]
            E.wait("vector", fl_sem[wsa])
            transpose_tiles4(
                [stag[:, hc * 128:(hc + 1) * 128] for hc in range(NHC)],
                lambda b, pb, wst=wst: E.eng.tensor_copy(
                    wst[:, b * 512:min((b + 1) * 512, NHC * 128)],
                    pb[:, 0:min(512, NHC * 128 - b * 512)]),
                extra_last=(("v2g", 1),))
            E.drain("tensor", sem="t2g")
            if oc + 2 < NKC:
                E.wait("gpsimd", "t2g")
                wd_load(oc + 2)
            E.wait("gpsimd", "v2g")
            src = wst[:, 0:NHC * 128].rearrange("p (k c) -> p k c", c=128)
            dst = wdq_s[oc, :, :, :].rearrange("k p c -> p k c")
            E.do("gpsimd", lambda e, dst=dst, src=src: e.dma_start(
                out=dst, in_=src), sem=fl_sem[wsa], n=16)
            fl_cnt[wsa] = E.c[fl_sem[wsa]]
            wsa ^= 1
        if tap == "wprep":
            E.wait("sync", "qGW")
            E.wait("sync", "qGW1")
            E.drain("vector", sem="v2q")
            E.wait("sync", "v2q")
            E.do("sync", lambda e: e.dma_start(out=dbg[:, 0:2816], in_=wd_stag[1]), sem="qO0", n=16)
            return

        sc_free = {2: ("s2t", 0), 3: ("s2t", 0), 6: ("s2t", 0), 7: ("s2t", 0)}
        av4_free = [("s2t", 0)]
        av5_free = [("s2t", 0)]
        SC_ROT = (2, 3, 6, 7)
        for qc in range(NQC):
            E.mark(f"A{qc}")
            for ttl in range(4):
                tt = qc * 4 + ttl
                xa, qs = load_x(tt, tt % 2)
                E.wait("scalar", qs)
                E.wait("scalar", "t2s")
                E.wait("scalar", "qO0" if ttl % 2 == 0 else "qO1")
                xd = otmp[:, (ttl % 2) * 1024:(ttl % 2) * 1024 + 1024]
                xn_tile(tt, xa, xd)
                E.drain("scalar", sem="s2t")
                E.wait("tensor", "s2t")
                if ttl == 0:
                    E.wait("tensor", "t2v")
                xnfm3 = xnfm.rearrange("p (k c) -> p k c", c=512)
                transpose_tiles4(
                    [xd[:, kc * 128:(kc + 1) * 128] for kc in range(NKC)],
                    lambda b, pb, ttl=ttl, xnfm3=xnfm3: E.eng.tensor_copy(
                        xnfm3[:, b * 4:b * 4 + 4, ttl * 128:ttl * 128 + 128],
                        pb[:, 0:512].rearrange("p (k c) -> p k c", c=128)))
                E.drain("tensor", sem="t2s")
            E.drain("vector", sem="v2t")
            E.mark(f"B{qc}")
            if qc == 0:
                E.wait("sync", "qGW")
                E.wait("sync", "qGW1")
                E.wait("scalar", "v2s")
            SLQ = ("qW0", "qW1", "qO0", "qO1")
            ev_cnt = [E.c["s2v"]] * 4
            ld_cnt = [0] * NKC

            def wq_load(i, W):
                wb = wbuf[:, (i % 4) * 1024:(i % 4) * 1024 + 1024]
                E.wait("sync", "s2v", ev_cnt[i % 4])
                E.do("sync", lambda e, i=i, wb=wb, W=W: e.dma_start(
                    out=wb.rearrange("p (k o) -> p k o", o=128),
                    in_=W[:, i * 128:(i + 1) * 128].rearrange("(k p) o -> p k o", p=128)),
                    sem=SLQ[i % 4], n=16)
                ld_cnt[i] = E.c[SLQ[i % 4]]
            wq_load(0, Wq)
            wq_load(1, Wq)
            for hc2 in range(NKC):
                if hc2 + 2 < NKC:
                    wq_load(hc2 + 2, Wq)
                wb = wbuf[:, (hc2 % 4) * 1024:(hc2 % 4) * 1024 + 1024]
                pb = ps[2 + hc2 % 2]
                E.wait("tensor", SLQ[hc2 % 4], ld_cnt[hc2])
                E.wait("tensor", "v2t")
                E.wait("tensor", "s2v")
                for kc in range(NKC):
                    last = kc == NKC - 1
                    E.do("tensor", lambda e, pb=pb, wb=wb, kc=kc: e.matmul(
                        pb[:, 0:512], wb[:, kc * 128:(kc + 1) * 128],
                        xnfm[:, kc * 512:(kc + 1) * 512],
                        start=(kc == 0), stop=last),
                        sem=("t2s" if last else None))
                E.wait("scalar", "t2s")
                E.do("scalar", lambda e, pb=pb, hc2=hc2: e.activation(
                    qfm[:, hc2 * 512:(hc2 + 1) * 512], pb[:, 0:512], AF.Identity,
                    scale=cst[:, 2:3], bias=biasf[:, hc2:hc2 + 1]), sem="s2v")
                ev_cnt[hc2 % 4] = E.c["s2v"]
            E.drain("scalar", sem="s2t")
            if tap == "q" and qc == 0:
                E.wait("sync", "s2t")
                E.do("sync", lambda e: e.dma_start(out=dbg[:, 0:4096], in_=qfm[:, :]), sem="qO0", n=16)
                return
            E.mark(f"C{qc}")
            E.wait("tensor", "s2v")
            av_sbA = ftmp[:, 0:512]
            av_sbB = ftmp[:, 512:1024]
            probsB = hqfm[:, 4096:6144]
            SCA = (2, 6)
            SCB = (3, 7)
            for hkp in (0, 2):
                chk = hkp // 2
                for g in range(4):
                    hA = g * NKV + hkp
                    hB = hA + 1
                    chq = hA // 2
                    ch_order = [qc] + list(range(qc))
                    nch = len(ch_order)
                    for ci, ch in enumerate(ch_order):
                        diag = ch == qc
                        first = ci == 0
                        last = ci == nch - 1
                        for j in range(4):
                            kc = ch * 4 + j
                            bA = SCA[j % 2]
                            bB = SCB[j % 2]
                            fsem, fval = sc_free[bA]
                            E.wait("tensor", fsem, fval)
                            E.do("tensor", lambda e, bA=bA, kc=kc, chk=chk, chq=chq: e.matmul(
                                ps[bA][:, 0:512],
                                kfm[0:64, chk * T + kc * 128: chk * T + kc * 128 + 128],
                                qfm[0:64, chq * 512:(chq + 1) * 512],
                                start=True, stop=True, tile_position=(0, 0)), sem="t2s")
                            fsem, fval = sc_free[bB]
                            E.wait("tensor", fsem, fval)
                            E.do("tensor", lambda e, bB=bB, kc=kc, chk=chk, chq=chq: e.matmul(
                                ps[bB][:, 0:512],
                                kfm[64:128, chk * T + kc * 128: chk * T + kc * 128 + 128],
                                qfm[64:128, chq * 512:(chq + 1) * 512],
                                start=True, stop=True, tile_position=(64, 0)), sem="t2s")
                            E.wait("scalar", "t2s")
                            if diag:
                                base = qc * 512 - kc * 128
                                E.do("scalar", lambda e, bA=bA, j=j: e.activation(
                                    probs[:, j * 512:(j + 1) * 512], ps[bA][:, 0:512], AF.Exp,
                                    bias=cst[:, 1:2]), sem="s2g")
                                sc_free[bA] = ("s2g", E.c["s2g"])
                                E.wait("gpsimd", "s2g")
                                E.do("gpsimd", lambda e, j=j, base=base: e.affine_select(
                                    probs[:, j * 512:(j + 1) * 512], probs[:, j * 512:(j + 1) * 512],
                                    pattern=[[1, 512]], compare_op=ALU.is_ge, fill=0.0,
                                    base=base, channel_multiplier=-1), sem="g2t")
                                E.do("scalar", lambda e, bB=bB, j=j: e.activation(
                                    probsB[:, j * 512:(j + 1) * 512], ps[bB][:, 0:512], AF.Exp,
                                    bias=cst[:, 1:2]), sem="s2g")
                                sc_free[bB] = ("s2g", E.c["s2g"])
                                E.wait("gpsimd", "s2g")
                                E.do("gpsimd", lambda e, j=j, base=base: e.affine_select(
                                    probsB[:, j * 512:(j + 1) * 512], probsB[:, j * 512:(j + 1) * 512],
                                    pattern=[[1, 512]], compare_op=ALU.is_ge, fill=0.0,
                                    base=base, channel_multiplier=-1), sem="g2t")
                            else:
                                E.do("scalar", lambda e, bA=bA, j=j: e.activation(
                                    probs[:, j * 512:(j + 1) * 512], ps[bA][:, 0:512], AF.Exp,
                                    bias=cst[:, 1:2]), sem="s2t")
                                sc_free[bA] = ("s2t", E.c["s2t"])
                                E.do("scalar", lambda e, bB=bB, j=j: e.activation(
                                    probsB[:, j * 512:(j + 1) * 512], ps[bB][:, 0:512], AF.Exp,
                                    bias=cst[:, 1:2]), sem="s2t")
                                sc_free[bB] = ("s2t", E.c["s2t"])
                        if diag:
                            E.wait("tensor", "g2t")
                        else:
                            E.wait("tensor", "s2t")
                        if first:
                            E.wait("tensor", av4_free[0][0], av4_free[0][1])
                            E.wait("tensor", av5_free[0][0], av5_free[0][1])
                        for j in range(4):
                            kc = ch * 4 + j
                            slot = (hkp * 16 + kc) * 65
                            E.do("tensor", lambda e, j=j, slot=slot, first=first, last=last: e.matmul(
                                ps[4][0:65, 0:512],
                                vT[:, slot:slot + 65],
                                probs[:, j * 512:(j + 1) * 512],
                                start=(first and j == 0), stop=(last and j == 3)))
                        for j in range(4):
                            kc = ch * 4 + j
                            slot = ((hkp + 1) * 16 + kc) * 65
                            E.do("tensor", lambda e, j=j, slot=slot, first=first, last=last: e.matmul(
                                ps[5][0:65, 0:512],
                                vT[:, slot:slot + 65],
                                probsB[:, j * 512:(j + 1) * 512],
                                start=(first and j == 0), stop=(last and j == 3)),
                                sem=("t2v" if (last and j == 3) else None))
                    E.wait("vector", "t2v")
                    E.do("vector", lambda e: e.tensor_copy(av_sbA[0:65, 0:512], ps[4][0:65, 0:512]),
                         sem="v2t")
                    av4_free[0] = ("v2t", E.c["v2t"])
                    copyA = E.c["v2t"]
                    E.do("vector", lambda e: e.tensor_copy(av_sbB[0:65, 0:512], ps[5][0:65, 0:512]),
                         sem="v2t")
                    av5_free[0] = ("v2t", E.c["v2t"])
                    copyB = E.c["v2t"]
                    for av_sb, bnk, cpc, h, scol in ((av_sbA, 0, copyA, hA, 64),
                                                     (av_sbB, 1, copyB, hB, 68)):
                        E.wait("tensor", "v2t", max(tbank_evac[bnk], cpc))
                        for qt in range(4):
                            E.do("tensor", lambda e, bnk=bnk, qt=qt, av_sb=av_sb: e.transpose(
                                ps[bnk][:, qt * 65:qt * 65 + 65],
                                av_sb[0:65, qt * 128:qt * 128 + 128],
                                ident[0:65, 0:65]),
                                sem=("t2v" if qt == 3 else None))
                        E.wait("vector", "t2v")
                        for qt in range(4):
                            E.do("vector", lambda e, bnk=bnk, qt=qt, scol=scol: e.reciprocal(
                                stats[:, scol + qt:scol + qt + 1],
                                ps[bnk][:, qt * 65 + 64:qt * 65 + 65]))
                        E.drain("vector")
                        for qt in range(4):
                            E.do("vector", lambda e, bnk=bnk, qt=qt, h=h, scol=scol: e.tensor_scalar(
                                attn[:, qt * 1024 + h * 64: qt * 1024 + h * 64 + 64],
                                ps[bnk][:, qt * 65:qt * 65 + 64],
                                stats[:, scol + qt:scol + qt + 1], None, ALU.mult),
                                sem="v2t")
                        tbank_evac[bnk] = E.c["v2t"]
            E.drain("vector", sem="v2s", also=(("v2t", 1),))
            if tap == "attn" and qc == 0:
                E.drain("vector", sem="v2q")
                E.wait("sync", "v2q")
                E.do("sync", lambda e: e.dma_start(out=dbg[:, 0:4096], in_=attn[:, :]), sem="qO0", n=16)
                return
            E.mark(f"D{qc}")
            for qt in range(4):
                E.do("vector", lambda e, qt=qt: e.reduce_sum(
                    stats[:, 64 + qt:65 + qt], attn[:, qt * 1024:(qt + 1) * 1024], axis=AX.X))
                E.wait("scalar", "v2s")
                E.do("scalar", lambda e, qt=qt: e.activation(
                    otmp[:, 0:1024], attn[:, qt * 1024:(qt + 1) * 1024], AF.Square,
                    accum_out=stats[:, 68 + qt:69 + qt]))
                E.drain("scalar")
            E.drain("scalar", sem="s2v")
            E.drain("vector")
            E.wait("vector", "s2v")
            E.do("vector", lambda e: e.tensor_single_scalar(stats[:, 72:76], stats[:, 64:68], 1.0 / 1024.0, ALU.mult))
            E.drain("vector")
            E.do("vector", lambda e: e.tensor_single_scalar(stats[:, 76:80], stats[:, 68:72], 1.0 / 1024.0, ALU.mult))
            E.drain("vector")
            E.do("vector", lambda e: e.tensor_tensor(stats[:, 80:84], stats[:, 72:76], stats[:, 72:76], ALU.mult))
            E.drain("vector")
            E.do("vector", lambda e: e.tensor_tensor(stats[:, 84:88], stats[:, 76:80], stats[:, 80:84], ALU.subtract))
            E.drain("vector", sem="v2s")
            E.wait("scalar", "v2s")
            E.do("scalar", lambda e: e.activation(stats[:, 88:92], stats[:, 84:88], AF.Sqrt, bias=cst[:, 0:1]))
            E.drain("scalar", sem="s2v")
            E.wait("vector", "s2v")
            E.do("vector", lambda e: e.reciprocal(stats[:, 92:96], stats[:, 88:92]))
            E.drain("vector")
            E.do("vector", lambda e: e.scalar_tensor_tensor(stats[:, 72:76], stats[:, 72:76], -1.0, stats[:, 92:96], ALU.mult, ALU.mult))
            E.drain("vector", sem="v2s")
            E.wait("scalar", "v2s")
            for qt in range(4):
                E.wait("scalar", "t2s")
                E.wait("scalar", "qO0" if qt % 2 == 0 else "qO1")
                xd = otmp[:, (qt % 2) * 1024:(qt % 2) * 1024 + 1024]
                E.do("scalar", lambda e, qt=qt, xd=xd: e.activation(
                    xd, attn[:, qt * 1024:(qt + 1) * 1024], AF.Identity,
                    scale=stats[:, 92 + qt:93 + qt], bias=stats[:, 72 + qt:73 + qt]))
                E.drain("scalar", sem="s2t")
                E.wait("tensor", "s2t")
                if qt == 0:
                    E.wait("tensor", "t2g")
                xnfm3 = xnfm.rearrange("p (k c) -> p k c", c=512)
                transpose_tiles4(
                    [xd[:, kc * 128:(kc + 1) * 128] for kc in range(NKC)],
                    lambda b, pb, qt=qt, xnfm3=xnfm3: E.eng.tensor_copy(
                        xnfm3[:, b * 4:b * 4 + 4, qt * 128:qt * 128 + 128],
                        pb[:, 0:512].rearrange("p (k c) -> p k c", c=128)))
                E.drain("tensor", sem="t2s")
            E.drain("vector")
            for kc in range(NKC):
                E.do("vector", lambda e, kc=kc: e.tensor_scalar(
                    xnfm[:, kc * 512:(kc + 1) * 512], xnfm[:, kc * 512:(kc + 1) * 512],
                    biasf[:, 20 + kc:21 + kc], biasf[:, 28 + kc:29 + kc],
                    ALU.mult, ALU.add), sem="v2t")
            E.drain("vector", sem="v2t")
            E.wait("vector", "t2s")
            xq_slots = ([hqfm[:, j * 512:(j + 1) * 512] for j in range(22)]
                        + [qfm[:, j * 512:(j + 1) * 512] for j in range(6)])
            for pp in range(2):
                xts = []
                for ci in range(2):
                    ttl = 2 * pp + ci
                    tt = qc * 4 + ttl
                    xa, qs = load_x(tt, tt % 2)
                    E.wait("scalar", qs)
                    xd = attn[:, ttl * 1024:(ttl + 1) * 1024]
                    xn_tile(tt, xa, xd)
                    xts.append(xd)
                E.drain("scalar")
                E.wait("scalar", "t2s")
                items = []
                si = 0
                for i_t, xd in enumerate(xts):
                    for half in range(2):
                        bi = xd[:, half * 512:half * 512 + 512]
                        ab = otmp[:, (2 * i_t + half) * 512:(2 * i_t + half + 1) * 512]
                        scr = [xq_slots[si + j] for j in range(7)]
                        si += 7
                        items.append((bi, ab, scr))
                quant_group(items)
            xq_done = E.c["v2s"]
            ev_cnt = [E.c["s2v"]] * 4
            ld_cnt = [0] * NKC
            wq_load(0, Wo)
            wq_load(1, Wo)
            for oc in range(NKC):
                if oc + 2 < NKC:
                    wq_load(oc + 2, Wo)
                wb = wbuf[:, (oc % 4) * 1024:(oc % 4) * 1024 + 1024]
                pb = ps[2 + oc % 2]
                E.wait("tensor", SLQ[oc % 4], ld_cnt[oc])
                E.wait("tensor", "v2t")
                E.wait("tensor", "s2v")
                for kc in range(NKC):
                    last = kc == NKC - 1
                    E.do("tensor", lambda e, pb=pb, wb=wb, kc=kc: e.matmul(
                        pb[:, 0:512], wb[:, kc * 128:(kc + 1) * 128],
                        xnfm[:, kc * 512:(kc + 1) * 512],
                        start=(kc == 0), stop=last),
                        sem=("t2s" if last else None))
                E.wait("scalar", "t2s")
                E.do("scalar", lambda e, pb=pb, oc=oc: e.activation(
                    sfm[:, oc * 512:(oc + 1) * 512], pb[:, 0:512], AF.Identity,
                    bias=biasf[:, 8 + oc:9 + oc]), sem="s2v")
                ev_cnt[oc % 4] = E.c["s2v"]
            E.drain("scalar", sem="s2t", also=(("s2v", 1),))
            if tap == "s" and qc == 0:
                E.wait("sync", "s2v")
                E.do("sync", lambda e: e.dma_start(out=dbg[:, 0:4096], in_=sfm[:, :]), sem="qO0", n=16)
                return
            E.wait("tensor", "v2s", xq_done)
            for ttl in range(4):
                xd = attn[:, ttl * 1024:(ttl + 1) * 1024]
                xnfm3 = xnfm.rearrange("p (k c) -> p k c", c=512)
                transpose_tiles4(
                    [xd[:, kc * 128:(kc + 1) * 128] for kc in range(NKC)],
                    lambda b, pb, ttl=ttl, xnfm3=xnfm3: E.eng.tensor_copy(
                        xnfm3[:, b * 4:b * 4 + 4, ttl * 128:ttl * 128 + 128],
                        pb[:, 0:512].rearrange("p (k c) -> p k c", c=128)))
            E.drain("tensor", sem="t2s")
            E.drain("vector", sem="v2t")
            if tap == "xq" and qc == 0:
                E.drain("vector", sem="v2q")
                E.wait("sync", "v2q")
                E.do("sync", lambda e: e.dma_start(out=dbg[:, 0:4096], in_=xnfm[:, :]), sem="qO0", n=16)
                return
            E.mark(f"E{qc}")
            E.wait("gpsimd", "qGW")
            E.wait("gpsimd", "qGW1")
            pg_free = [E.c["s2v"], E.c["s2v"]]
            pu_free = [E.c["v2t"], E.c["v2t"]]

            def h_quant(hbq):
                h0q, hwq = HBS[hbq]
                h2 = hwq // 2
                colq = attn if hbq % 2 == 0 else probs
                nsub_scr = ([qfm[:, j * 256:(j + 1) * 256] for j in range(16)]
                            + [otmp[:, j * 256:(j + 1) * 256] for j in range(8)]
                            + [attn[:, 3072 + j * 256:3072 + (j + 1) * 256] for j in range(4)])
                for grp in range(2):
                    items = []
                    for t in range(2):
                        qt = grp * 2 + t
                        for half in range(2):
                            bi = colq[:, qt * hwq + half * h2: qt * hwq + half * h2 + h2]
                            ab = attn[:, 2048 + (2 * t + half) * 256: 2048 + (2 * t + half) * 256 + h2]
                            scr = [nsub_scr[(2 * t + half) * 7 + j][:, 0:h2] for j in range(7)]
                            items.append((bi, ab, scr))
                    quant_group(items)

            def h_transpose(hbq):
                h0q, hwq = HBS[hbq]
                colq = attn if hbq % 2 == 0 else probs
                E.drain("vector", sem="v2t")
                E.wait("tensor", "v2t")
                srcs = []
                for qt in range(4):
                    for i in range(hwq // 128):
                        srcs.append(colq[:, qt * hwq + i * 128: qt * hwq + i * 128 + 128])
                nbq = hwq // 128
                hq3 = hqfm.rearrange("p (k c) -> p k c", c=512)

                def hq_evac(b, pb, hq3=hq3, h0q=h0q, nbq=nbq):
                    return E.eng.tensor_copy(
                        hq3[:, h0q // 128:h0q // 128 + nbq, b * 128:b * 128 + 128],
                        pb[:, 0:nbq * 128].rearrange("p (k c) -> p k c", c=128))
                transpose_tiles4(srcs, hq_evac, nb=nbq)
                E.drain("tensor", sem="t2s")

            for hb, (h0, hw) in enumerate(HBS):
                E.wait("gpsimd", "s2v", max(pg_free))
                E.wait("gpsimd", "v2t", max(pu_free))
                E.do("gpsimd", lambda e, hb=hb, hw=hw: e.dma_start(
                    out=ffnw[:, 0:NKC * hw].rearrange("p (k c) -> p k c", c=hw),
                    in_=wgq_s[hb, :, :, 0:hw].rearrange("k p c -> p k c")), sem="qG", n=16)
                E.do("gpsimd", lambda e, hb=hb, hw=hw: e.dma_start(
                    out=ffnw[:, 4096:4096 + NKC * hw].rearrange("p (k c) -> p k c", c=hw),
                    in_=wuq_s[hb, :, :, 0:hw].rearrange("k p c -> p k c")), sem="qG1", n=16)
                if hb > 0:
                    h_quant(hb - 1)
                colb = attn if hb % 2 == 0 else probs
                for qt in range(4):
                    pg = ps[2 + 2 * (qt % 2)]
                    pu = ps[3 + 2 * (qt % 2)]
                    par = qt % 2
                    E.wait("tensor", "qG")
                    E.wait("tensor", "qG1")
                    E.wait("tensor", "s2v", pg_free[par])
                    E.wait("tensor", "v2t", pu_free[par])
                    for kc in range(NKC):
                        last = kc == NKC - 1
                        E.do("tensor", lambda e, pg=pg, kc=kc, qt=qt, hw=hw: e.matmul(
                            pg[:, 0:hw], xnfm[:, kc * 512 + qt * 128: kc * 512 + qt * 128 + 128],
                            ffnw[:, kc * hw:(kc + 1) * hw],
                            start=(kc == 0), stop=last),
                            sem=("t2s" if last else None))
                    for kc in range(NKC):
                        last = kc == NKC - 1
                        E.do("tensor", lambda e, pu=pu, kc=kc, qt=qt, hw=hw: e.matmul(
                            pu[:, 0:hw], xnfm[:, kc * 512 + qt * 128: kc * 512 + qt * 128 + 128],
                            ffnw[:, 4096 + kc * hw:4096 + (kc + 1) * hw],
                            start=(kc == 0), stop=last),
                            sem=("t2v" if last else None))
                    E.wait("scalar", "t2s")
                    E.wait("scalar", "v2t", pu_free[par])
                    sg = ftmp[:, par * 512:par * 512 + hw]
                    E.do("scalar", lambda e, pg=pg, sg=sg, hw=hw: e.activation(
                        sg, pg[:, 0:hw], AF.Silu))
                    E.drain("scalar", sem="s2v")
                    pg_free[par] = E.c["s2v"]
                    E.wait("vector", "t2v")
                    E.wait("vector", "s2v")
                    E.do("vector", lambda e, pu=pu, sg=sg, qt=qt, hw=hw, colb=colb: e.tensor_tensor(
                        colb[:, qt * hw:(qt + 1) * hw], sg, pu[:, 0:hw], ALU.mult), sem="v2t")
                    pu_free[par] = E.c["v2t"]
                if hb > 0:
                    h_transpose(hb - 1)
            h_quant(len(HBS) - 1)
            h_transpose(len(HBS) - 1)
            E.drain("vector", sem="v2t")
            sg_free = [E.c["v2s"]]
            hq_done = E.c["v2t"]
            dn_free = [max(pu_free), max(pu_free)]
            dn_silu = [max(pg_free), max(pg_free)]
            for oc in range(NKC):
                E.wait("gpsimd", "v2t", max(max(pu_free), hq_done))
                E.wait("gpsimd", "s2v", max(dn_silu))
                E.do("gpsimd", lambda e, oc=oc: e.dma_start(
                    out=wdbuf.rearrange("p (k c) -> p k c", c=128),
                    in_=wdq_s[oc, :, :, :].rearrange("k p c -> p k c")), sem="qG", n=16)
                pd = ps[2 + oc % 2]
                E.wait("tensor", "qG")
                E.wait("tensor", "qG1")
                E.wait("tensor", "v2t", max(max(pu_free), hq_done))
                E.wait("tensor", "s2v", dn_silu[oc % 2])
                for hc in range(NHC):
                    last = hc == NHC - 1
                    E.do("tensor", lambda e, pd=pd, hc=hc: e.matmul(
                        pd[:, 0:512], wdbuf[:, hc * 128:(hc + 1) * 128],
                        hqfm[:, hc * 512:(hc + 1) * 512],
                        start=(hc == 0), stop=last),
                        sem=("t2s" if last else None))
                E.wait("scalar", "t2s")
                E.wait("scalar", "v2s", sg_free[0])
                sgb = ftmp[:, (oc % 2) * 512:(oc % 2) * 512 + 512]
                E.do("scalar", lambda e, pd=pd, sgb=sgb: e.activation(
                    sgb, pd[:, 0:512], AF.Silu))
                E.drain("scalar", sem="s2v")
                dn_silu[oc % 2] = E.c["s2v"]
                E.wait("vector", "s2v")
                E.do("vector", lambda e, oc=oc, sgb=sgb: e.tensor_tensor(
                    sfm[:, oc * 512:(oc + 1) * 512], sgb,
                    sfm[:, oc * 512:(oc + 1) * 512], ALU.mult), sem="v2t", also=(("v2s", 1),))
                dn_free[oc % 2] = E.c["v2t"]
                sg_free[0] = E.c["v2s"]
            sc_free[2] = ("s2v", E.c["s2v"])
            sc_free[3] = ("s2v", E.c["s2v"])
            av4_free[0] = ("s2v", E.c["s2v"])
            av5_free[0] = ("v2t", E.c["v2t"])
            E.drain("vector", sem="v2t")
            if tap == "sg" and qc == 0:
                E.drain("vector", sem="v2q")
                E.wait("sync", "v2q")
                E.do("sync", lambda e: e.dma_start(out=dbg[:, 0:4096], in_=sfm[:, :]), sem="qO0", n=16)
                return
            E.mark(f"F{qc}")
            for ttl in range(4):
                tt = qc * 4 + ttl
                xa, qs = load_x(tt, tt % 2)
                ob = otmp[:, (ttl % 2) * 1024:(ttl % 2) * 1024 + 1024]
                qo = "qO0" if ttl % 2 == 0 else "qO1"
                E.wait("vector", qs)
                E.wait("vector", qo)
                E.wait("tensor", "v2t")
                transpose_tiles4(
                    [sfm[:, oc * 512 + ttl * 128: oc * 512 + ttl * 128 + 128] for oc in range(NKC)],
                    lambda b, pb, xa=xa, ob=ob: E.eng.tensor_tensor(
                        ob[:, b * 512:(b + 1) * 512], pb[:, 0:512],
                        xa[:, b * 512:(b + 1) * 512], ALU.add),
                    extra_last=(("v2q", 1),))
                E.wait("sync", "v2q")
                E.do("sync", lambda e, tt=tt, ob=ob: e.dma_start(
                    out=y[tt * 128:(tt + 1) * 128, :], in_=ob), sem=qo, n=16)
            E.drain("vector", sem="v2t")
        E.mark("END")

    if sim_check:
        logs = {}
        for en in ("sync", "tensor", "vector", "scalar", "gpsimd"):
            lg = []
            Es = EV(en, None, sems, log=lg)
            schedule(Es)
            for s_ in ("qC", "qX0", "qX1", "qW0", "qW1", "qO0", "qO1", "qG", "qGW"):
                Es.wait(en, s_)
            logs[en] = lg
        _check_deadlock(logs)

    with nc.Block() as block:
        def run(en, eng):
            Ei = EV(en, eng, sems)
            schedule(Ei)
            for s_ in ("qC", "qX0", "qX1", "qW0", "qW1", "qO0", "qO1", "qG", "qGW"):
                Ei.wait(en, s_)

        @block.sync
        def _(eng):
            run("sync", eng)

        @block.tensor
        def _(eng):
            run("tensor", eng)

        @block.vector
        def _(eng):
            run("vector", eng)

        @block.scalar
        def _(eng):
            run("scalar", eng)

        @block.gpsimd
        def _(eng):
            run("gpsimd", eng)

    cx.close()
    return nc


_NC_CACHE = {}


def kernel(L0_attn_x, L0_attn_Wq, L0_attn_bq, L0_attn_Wk, L0_attn_bk,
           L0_attn_Wv, L0_attn_bv, L0_attn_ln_g, L0_attn_ln_b, L0_attn_Wo,
           L0_attn_bo, L0_ffn_w_gate, L0_ffn_w_up, L0_ffn_w_down):
    xs = np.ascontiguousarray(np.asarray(L0_attn_x, dtype=np.float32))
    w = {
        "Wq": np.ascontiguousarray(np.asarray(L0_attn_Wq, dtype=np.float32)),
        "bq": np.ascontiguousarray(np.asarray(L0_attn_bq, dtype=np.float32)),
        "Wk": np.ascontiguousarray(np.asarray(L0_attn_Wk, dtype=np.float32)),
        "bk": np.ascontiguousarray(np.asarray(L0_attn_bk, dtype=np.float32)),
        "Wv": np.ascontiguousarray(np.asarray(L0_attn_Wv, dtype=np.float32)),
        "bv": np.ascontiguousarray(np.asarray(L0_attn_bv, dtype=np.float32)),
        "lng": np.ascontiguousarray(np.asarray(L0_attn_ln_g, dtype=np.float32)),
        "lnb": np.ascontiguousarray(np.asarray(L0_attn_ln_b, dtype=np.float32)),
        "Wo": np.ascontiguousarray(np.asarray(L0_attn_Wo, dtype=np.float32)),
        "bo": np.ascontiguousarray(np.asarray(L0_attn_bo, dtype=np.float32)),
        "wg": np.ascontiguousarray(np.asarray(L0_ffn_w_gate, dtype=np.float32)),
        "wu": np.ascontiguousarray(np.asarray(L0_ffn_w_up, dtype=np.float32)),
        "wd": np.ascontiguousarray(np.asarray(L0_ffn_w_down, dtype=np.float32)),
    }
    if "nc" not in _NC_CACHE:
        _NC_CACHE["nc"] = build_nc()
    nc = _NC_CACHE["nc"]
    in_maps = [dict(w, x=np.ascontiguousarray(xs[c])) for c in range(8)]
    res = run_bass_kernel_spmd(nc, in_maps, core_ids=list(range(8)))
    return np.stack([np.asarray(res.results[c]["y"], dtype=np.float32) for c in range(8)], axis=0)


# version 30 (final)
# speedup vs baseline: 1.0687x; 1.0687x over previous
import contextlib

import numpy as np

import concourse.bass as bass
import concourse.mybir as mybir
from concourse.bass_utils import run_bass_kernel_spmd

F32 = mybir.dt.float32
I32 = mybir.dt.int32
AF = mybir.ActivationFunctionType
ALU = mybir.AluOpType
AX = mybir.AxisListType

T = 2048
D = 1024
DKV = 256
NKV = 4
DI = 2816
NKC = D // 128
NTT = T // 128
NQC = T // 512
NHC = DI // 128
HBS = [(0, 512), (512, 512), (1024, 512), (1536, 512), (2048, 512), (2560, 256)]
R6 = float(np.float32(1.0) / np.float32(6.0))

SEMS = ["qC", "qX0", "qX1", "qW0", "qW1", "qO0", "qO1", "qG", "qG1", "qP0", "qP1", "qGW", "qGW1",
        "t2v", "t2s", "t2g", "v2t", "v2s", "v2q", "v2g", "vv",
        "s2t", "s2v", "s2q", "s2g", "g2t"]


class EV:

    def __init__(self, name, eng, sems, log=None):
        self.name = name
        self.eng = eng
        self.sems = sems
        self.c = {k: 0 for k in SEMS}
        self.log = log
        self.label = ""

    def mark(self, label):
        self.label = label

    def wait(self, on_eng, sem, val=None):
        v = self.c[sem] if val is None else val
        if self.name == on_eng:
            if self.log is not None:
                self.log.append(("wait", sem, v, self.label))
            if self.eng is not None:
                self.eng.wait_ge(self.sems[sem], v)

    def _marker(self, e):
        if self.name == "tensor":
            return e.nop()
        if self.name in ("vector", "gpsimd"):
            return e.engine_nop()
        if self.name == "scalar":
            return e.drain()
        return e.nop()

    def do(self, on_eng, fn, sem=None, n=1, also=()):
        if sem is not None:
            self.c[sem] += n
        if self.name == on_eng:
            if self.log is not None and sem is not None:
                self.log.append(("inc", sem, self.c[sem], self.label))
            if self.log is not None and sem is None:
                self.log.append(("inc", None, None, self.label))
            if self.eng is not None:
                ins = fn(self.eng)
                if sem is not None:
                    ins.then_inc(self.sems[sem], n)
        for s2, n2 in also:
            self.c[s2] += n2
            if self.name == on_eng:
                if self.log is not None:
                    self.log.append(("inc", s2, self.c[s2], self.label))
                if self.eng is not None:
                    self._marker(self.eng).then_inc(self.sems[s2], n2)

    def drain(self, on_eng, sem=None, n=1, also=()):
        self.do(on_eng, lambda e: e.drain(), sem=sem, n=n, also=also)


def _check_deadlock(logs):
    pos = {e: 0 for e in logs}
    cnt = {s: 0 for s in SEMS}
    progressed = True
    while progressed:
        progressed = False
        for e, ops in logs.items():
            while pos[e] < len(ops):
                kind, sem, val, label = ops[pos[e]]
                if kind == "wait" and sem is not None and cnt[sem] < val:
                    break
                if kind == "inc" and sem is not None:
                    cnt[sem] = max(cnt[sem], val)
                pos[e] += 1
                progressed = True
    stuck = {e: (pos[e], len(ops), logs[e][pos[e]]) for e, ops in logs.items()
             if pos[e] < len(ops)}
    if stuck:
        raise RuntimeError(f"schedule deadlock: {stuck}")


def build_nc(tap=None, sim_check=True):
    nc = bass.Bass("TRN2", target_bir_lowering=False)

    x = nc.dram_tensor("x", [T, D], F32, kind="ExternalInput").ap()
    Wq = nc.dram_tensor("Wq", [D, D], F32, kind="ExternalInput").ap()
    bq = nc.dram_tensor("bq", [D], F32, kind="ExternalInput").ap()
    Wk = nc.dram_tensor("Wk", [D, DKV], F32, kind="ExternalInput").ap()
    bk = nc.dram_tensor("bk", [DKV], F32, kind="ExternalInput").ap()
    Wv = nc.dram_tensor("Wv", [D, DKV], F32, kind="ExternalInput").ap()
    bv = nc.dram_tensor("bv", [DKV], F32, kind="ExternalInput").ap()
    lng = nc.dram_tensor("lng", [D], F32, kind="ExternalInput").ap()
    lnb = nc.dram_tensor("lnb", [D], F32, kind="ExternalInput").ap()
    Wo = nc.dram_tensor("Wo", [D, D], F32, kind="ExternalInput").ap()
    bo = nc.dram_tensor("bo", [D], F32, kind="ExternalInput").ap()
    wg = nc.dram_tensor("wg", [DI, D], F32, kind="ExternalInput").ap()
    wu = nc.dram_tensor("wu", [DI, D], F32, kind="ExternalInput").ap()
    wd = nc.dram_tensor("wd", [D, DI], F32, kind="ExternalInput").ap()
    y = nc.dram_tensor("y", [T, D], F32, kind="ExternalOutput").ap()
    dbg = None
    if tap is not None:
        dbg = nc.dram_tensor("dbg", [128, 24576], F32, kind="ExternalOutput").ap()

    wgq_s = nc.dram_tensor("wgq_s", [len(HBS), NKC, 128, 512], F32, kind="Internal").ap()
    wuq_s = nc.dram_tensor("wuq_s", [len(HBS), NKC, 128, 512], F32, kind="Internal").ap()
    wdq_s = nc.dram_tensor("wdq_s", [NKC, NHC, 128, 128], F32, kind="Internal").ap()

    ident_d = nc.inline_tensor(np.eye(128, dtype=np.float32), name="ident_c").ap()
    cvals = np.zeros((128, 8), dtype=np.float32)
    cvals[:, 0] = 1e-5
    cvals[:, 1] = -20.0
    cvals[:, 2] = 0.125
    cst_d = nc.inline_tensor(cvals, name="cst_c").ap()

    cx = contextlib.ExitStack()
    sb = lambda nm, shape: cx.enter_context(nc.sbuf_tensor(nm, shape, F32))

    ident = sb("ident", [128, 128])
    cst = sb("cst", [128, 8])
    biasf = sb("biasf", [128, 40])
    stats = sb("stats", [128, 96])
    qstat = sb("qstat", [128, 512])
    xbuf = sb("xbuf", [128, 2 * 1024])
    kfm = sb("kfm", [128, 2 * T])
    vT = sb("vT", [128, NKV * 16 * 65])
    qfm = sb("qfm", [128, NKC * 512])
    xnfm = sb("xnfm", [128, NKC * 512])
    attn = sb("attn", [128, 4 * 1024])
    sfm = sb("sfm", [128, NKC * 512])
    probs = sb("probs", [128, 4 * 512])
    ffnw = sb("ffnw", [128, 2 * 4096])
    hqfm = sb("hqfm", [128, NHC * 512])
    vfm = hqfm[:, 0:2 * T]
    otmp = sb("otmp", [128, 2 * 1024])
    ftmp = sb("ftmp", [128, 2 * 512])

    wbuf = ffnw[:, 0:4096]
    wdbuf = ffnw[:, 0:NHC * 128]
    wstage = ffnw[:, 4096:8192]

    ps = [cx.enter_context(nc.psum_tensor(f"p{i}", [128, 512], F32)) for i in range(8)]
    sems = {nm: cx.enter_context(nc.semaphore(nm)) for nm in SEMS}

    def schedule(E: EV):
        qbm = qstat[:, 0:64]
        qrs = qstat[:, 64:128]

        E.mark("P0")
        E.do("sync", lambda e: e.dma_start(out=ident[:, :], in_=ident_d[:, :]), sem="qC", n=16)
        E.do("sync", lambda e: e.dma_start(out=cst[:, :], in_=cst_d[:, :]), sem="qC", n=16)
        E.do("sync", lambda e: e.dma_start(out=biasf[:, 0:8], in_=bq.rearrange("(o p) -> p o", p=128), allow_slow_non_contiguous=True), sem="qC", n=16)
        E.do("sync", lambda e: e.dma_start(out=biasf[:, 8:16], in_=bo.rearrange("(o p) -> p o", p=128), allow_slow_non_contiguous=True), sem="qC", n=16)
        E.do("sync", lambda e: e.dma_start(out=biasf[:, 16:18], in_=bk.rearrange("(o p) -> p o", p=128), allow_slow_non_contiguous=True), sem="qC", n=16)
        E.do("sync", lambda e: e.dma_start(out=biasf[:, 18:20], in_=bv.rearrange("(o p) -> p o", p=128), allow_slow_non_contiguous=True), sem="qC", n=16)
        E.do("sync", lambda e: e.dma_start(out=biasf[:, 20:28], in_=lng.rearrange("(o p) -> p o", p=128), allow_slow_non_contiguous=True), sem="qC", n=16)
        E.do("sync", lambda e: e.dma_start(out=biasf[:, 28:36], in_=lnb.rearrange("(o p) -> p o", p=128), allow_slow_non_contiguous=True), sem="qC", n=16)
        c0 = E.c["qC"]
        E.wait("vector", "qC", c0)
        E.do("vector", lambda e: e.tensor_single_scalar(biasf[:, 0:8], biasf[:, 0:8], 0.125, ALU.mult))
        for i in range(NKV * 16):
            col = i * 65 + 64
            E.do("vector", lambda e, col=col: e.memset(vT[:, col:col + 1], 1.0))
        E.drain("vector", sem="v2s", also=(("v2t", 1),))
        E.wait("scalar", "qC", c0)
        E.wait("scalar", "v2s")
        E.wait("tensor", "qC", c0)
        E.wait("tensor", "v2t")

        def load_x(tt, b):
            xa = xbuf[:, b * 1024:(b + 1) * 1024]
            qs = "qX0" if b == 0 else "qX1"
            E.wait("sync", "v2q")
            E.wait("sync", "s2q")
            E.do("sync", lambda e, tt=tt, xa=xa: e.dma_start(
                out=xa, in_=x[tt * 128:(tt + 1) * 128, :]), sem=qs, n=16)
            return xa, qs

        E.mark("P1")
        for tt in range(NTT):
            xa, qs = load_x(tt, tt % 2)
            E.wait("vector", qs)
            E.do("vector", lambda e, tt=tt, xa=xa: e.reduce_sum(
                stats[:, 32 + tt:33 + tt], xa, axis=AX.X), sem="v2q")
            E.wait("scalar", qs)
            E.do("scalar", lambda e, tt=tt, xa=xa: e.activation(
                otmp[:, 0:1024], xa, AF.Square, accum_out=stats[:, 48 + tt:49 + tt]), sem="s2q")
        E.drain("scalar", sem="s2v")
        E.wait("vector", "s2v")
        E.drain("vector")
        E.do("vector", lambda e: e.tensor_single_scalar(stats[:, 64:80], stats[:, 32:48], 1.0 / 1024.0, ALU.mult))
        E.drain("vector")
        E.do("vector", lambda e: e.tensor_single_scalar(stats[:, 32:48], stats[:, 48:64], 1.0 / 1024.0, ALU.mult))
        E.drain("vector")
        E.do("vector", lambda e: e.tensor_tensor(stats[:, 80:96], stats[:, 64:80], stats[:, 64:80], ALU.mult))
        E.drain("vector")
        E.do("vector", lambda e: e.tensor_tensor(stats[:, 48:64], stats[:, 32:48], stats[:, 80:96], ALU.subtract))
        E.drain("vector", sem="v2s")
        E.wait("scalar", "v2s")
        E.do("scalar", lambda e: e.activation(stats[:, 32:48], stats[:, 48:64], AF.Sqrt, bias=cst[:, 0:1]))
        E.drain("scalar", sem="s2v")
        E.wait("vector", "s2v")
        E.do("vector", lambda e: e.reciprocal(stats[:, 0:16], stats[:, 32:48]))
        E.drain("vector")
        E.do("vector", lambda e: e.scalar_tensor_tensor(stats[:, 16:32], stats[:, 64:80], -1.0, stats[:, 0:16], ALU.mult, ALU.mult))
        E.drain("vector", sem="v2s", also=(("v2t", 1), ("v2q", 1)))
        E.wait("scalar", "v2s")
        if tap == "stats":
            E.wait("sync", "v2q")
            E.do("sync", lambda e: e.dma_start(out=dbg[:, 0:96], in_=stats[:, 0:96]), sem="qO0", n=16)
            return

        def xn_tile(tt, xa, dst):
            E.do("scalar", lambda e, tt=tt, xa=xa, dst=dst: e.activation(
                dst, xa, AF.Identity, scale=stats[:, tt:tt + 1],
                bias=stats[:, 16 + tt:17 + tt]), sem="s2q")

        def quant_group(items):
            n = len(items)
            Fs = [it[0].free_size() for it in items]
            nbs = [f // 16 for f in Fs]
            assert sum(nbs) <= 256, nbs
            bmo = [0] * n
            off = 0
            for i, nb in enumerate(nbs):
                bmo[i] = off
                off += nb
            bms = [qstat[:, 2 * bmo[i]:2 * bmo[i] + nbs[i]] for i in range(n)]
            rss = [qstat[:, 256 + 2 * bmo[i]:256 + 2 * bmo[i] + nbs[i]] for i in range(n)]
            bms = [qstat[:, bmo[i]:bmo[i] + nbs[i]] for i in range(n)]
            rss = [qstat[:, 256 + bmo[i]:256 + bmo[i] + nbs[i]] for i in range(n)]
            E.wait("scalar", "v2s")
            E.wait("scalar", "v2t")
            for (bi, ba, _s), F in zip(items, Fs):
                E.do("scalar", lambda e, bi=bi, ba=ba: e.activation(ba, bi, AF.Abs))
            E.drain("scalar", sem="s2v")
            E.wait("vector", "s2v")
            a3s = [ba.rearrange("p (b k) -> p b k", k=16) for (_b, ba, _s) in items]
            for i in range(n):
                E.do("vector", lambda e, i=i: e.reduce_max(bms[i], a3s[i], axis=AX.X))
            E.drain("vector")
            for i in range(n):
                E.do("vector", lambda e, i=i: e.tensor_scalar(
                    bms[i], bms[i], R6, 1e-12, ALU.mult, ALU.max))
            E.drain("vector")
            for i in range(n):
                E.do("vector", lambda e, i=i: e.reciprocal(rss[i], bms[i]))
            E.drain("vector")
            rs_b = [rss[i].unsqueeze(2).broadcast_to([128, nbs[i], 16]) for i in range(n)]
            bm_b = [bms[i].unsqueeze(2).broadcast_to([128, nbs[i], 16]) for i in range(n)]
            for i in range(n):
                E.do("vector", lambda e, i=i: e.tensor_tensor(a3s[i], a3s[i], rs_b[i], ALU.mult))
            E.drain("vector")
            for j, thr in enumerate((0.25, 0.75, 1.25, 1.75, 2.5, 3.5, 5.0)):
                for i, (bi, ba, scr) in enumerate(items):
                    E.do("vector", lambda e, i=i, j=j, thr=thr, ba=ba, scr=scr: e.tensor_single_scalar(
                        scr[j], ba, thr, ALU.is_gt))
            E.drain("vector")
            for a, bdx in ((0, 1), (2, 3), (4, 5)):
                for i, (bi, ba, scr) in enumerate(items):
                    E.do("vector", lambda e, scr=scr, a=a, bdx=bdx: e.tensor_tensor(
                        scr[a], scr[a], scr[bdx], ALU.add))
            E.drain("vector")
            for i, (bi, ba, scr) in enumerate(items):
                E.do("vector", lambda e, scr=scr: e.tensor_tensor(scr[0], scr[0], scr[2], ALU.add))
            for i, (bi, ba, scr) in enumerate(items):
                E.do("vector", lambda e, scr=scr: e.scalar_tensor_tensor(scr[4], scr[6], 2.0, scr[4], ALU.mult, ALU.add))
            for i, (bi, ba, scr) in enumerate(items):
                E.do("vector", lambda e, bi=bi, scr=scr: e.tensor_single_scalar(
                    scr[1].bitcast(I32), bi.bitcast(I32), -0x80000000, ALU.bitwise_and))
            E.drain("vector")
            for i, (bi, ba, scr) in enumerate(items):
                E.do("vector", lambda e, scr=scr: e.scalar_tensor_tensor(scr[0], scr[0], 0.5, scr[4], ALU.mult, ALU.add))
            E.drain("vector")
            for i, (bi, ba, scr) in enumerate(items):
                E.do("vector", lambda e, scr=scr: e.tensor_tensor(
                    scr[0].bitcast(I32), scr[0].bitcast(I32), scr[1].bitcast(I32), ALU.bitwise_or))
            E.drain("vector")
            for i, (bi, ba, scr) in enumerate(items):
                o3 = items[i][0].rearrange("p (b k) -> p b k", k=16)
                c3 = items[i][2][0].rearrange("p (b k) -> p b k", k=16)
                E.do("vector", lambda e, i=i, o3=o3, c3=c3: e.tensor_tensor(o3, c3, bm_b[i], ALU.mult),
                     sem=("v2s" if i == n - 1 else None))

        tbank_evac = {0: 0, 1: 0}

        def transpose_tiles(srcs, dst_fn, rows=128, extra_last=()):
            for i in range(len(srcs)):
                bk = i % 2
                pb = ps[bk]
                E.wait("tensor", "v2t", tbank_evac[bk])
                b = srcs[i].base_partition()
                E.do("tensor", lambda e, i=i, pb=pb, b=b: e.transpose(
                    pb[:, 0:rows], srcs[i], ident[b:b + rows, b:b + rows]), sem="t2v")
                E.wait("vector", "t2v")
                E.do("vector", lambda e, i=i, pb=pb: dst_fn(i, pb[:, 0:rows]),
                     sem="v2t")
                tbank_evac[bk] = E.c["v2t"]
            for s2, n2 in extra_last:
                E.drain("vector", sem=s2, n=n2)

        def transpose_tiles4(srcs, dst4_fn, rows=128, nb=4, extra_last=()):
            nbat = (len(srcs) + nb - 1) // nb
            for b in range(nbat):
                bk = b % 2
                pb = ps[bk]
                batch = srcs[b * nb:(b + 1) * nb]
                E.wait("tensor", "v2t", tbank_evac[bk])
                for j, s in enumerate(batch):
                    bp = s.base_partition()
                    E.do("tensor", lambda e, s=s, pb=pb, j=j, bp=bp, rows=rows: e.transpose(
                        pb[:, j * rows:(j + 1) * rows], s, ident[bp:bp + rows, bp:bp + rows]),
                        sem=("t2v" if j == len(batch) - 1 else None))
                E.wait("vector", "t2v")
                E.do("vector", lambda e, b=b, pb=pb: dst4_fn(b, pb), sem="v2t")
                tbank_evac[bk] = E.c["v2t"]
            for s2, n2 in extra_last:
                E.drain("vector", sem=s2, n=n2)

        E.mark("P2")
        wkv = ffnw[:, 0:2048]
        wvv = ffnw[:, 2048:4096]
        E.wait("sync", "qGW")
        E.wait("sync", "qGW1")
        E.do("sync", lambda e: e.dma_start(out=wkv.rearrange("p (k o) -> p k o", o=DKV),
                                           in_=Wk.rearrange("(k p) o -> p k o", p=128)), sem="qC", n=16)
        E.do("sync", lambda e: e.dma_start(out=wvv.rearrange("p (k o) -> p k o", o=DKV),
                                           in_=Wv.rearrange("(k p) o -> p k o", p=128)), sem="qC", n=16)
        c_wkv = E.c["qC"]
        for tc in range(NQC):
            E.mark(f"P2.t{tc}")
            for ttl in range(4):
                tt = tc * 4 + ttl
                xa, qs = load_x(tt, tt % 2)
                E.wait("scalar", qs)
                E.wait("scalar", "t2s")
                E.wait("scalar", "qO0" if ttl % 2 == 0 else "qO1")
                xd = otmp[:, (ttl % 2) * 1024:(ttl % 2) * 1024 + 1024]
                xn_tile(tt, xa, xd)
                E.drain("scalar", sem="s2t")
                E.wait("tensor", "s2t")
                if tc > 0 and ttl == 0:
                    E.wait("tensor", "s2v")
                xnfm3 = xnfm.rearrange("p (k c) -> p k c", c=512)
                transpose_tiles4(
                    [xd[:, kc * 128:(kc + 1) * 128] for kc in range(NKC)],
                    lambda b, pb, ttl=ttl, xnfm3=xnfm3: E.eng.tensor_copy(
                        xnfm3[:, b * 4:b * 4 + 4, ttl * 128:ttl * 128 + 128],
                        pb[:, 0:512].rearrange("p (k c) -> p k c", c=128)))
                E.drain("tensor", sem="t2s")
            E.drain("vector", sem="v2t")
            E.wait("tensor", "v2t")
            E.wait("tensor", "qC", c_wkv)
            for wsb, bcol, dst in ((wkv, 16, kfm), (wvv, 18, vfm)):
                for oc in range(2):
                    pb = ps[2 + oc]
                    E.wait("tensor", "s2v")
                    for kc in range(NKC):
                        E.do("tensor", lambda e, pb=pb, wsb=wsb, oc=oc, kc=kc: e.matmul(
                            pb[:, 0:512], wsb[:, kc * 256 + oc * 128: kc * 256 + oc * 128 + 128],
                            xnfm[:, kc * 512:(kc + 1) * 512],
                            start=(kc == 0), stop=(kc == NKC - 1)),
                            sem=("t2s" if kc == NKC - 1 else None))
                    E.wait("scalar", "t2s")
                    E.do("scalar", lambda e, pb=pb, dst=dst, bcol=bcol, oc=oc, tc=tc: e.activation(
                        dst[:, oc * T + tc * 512: oc * T + tc * 512 + 512], pb[:, 0:512],
                        AF.Identity, bias=biasf[:, bcol + oc:bcol + oc + 1]), sem="s2v")
        E.drain("scalar", sem="s2t", also=(("s2v", 1), ("s2q", 1)))
        if tap == "kv":
            E.wait("sync", "s2v")
            E.do("sync", lambda e: e.dma_start(out=dbg[:, 0:4096], in_=kfm[:, :]), sem="qO0", n=16)
            E.do("sync", lambda e: e.dma_start(out=dbg[:, 4096:8192], in_=vfm[:, :]), sem="qO1", n=16)
            return

        E.mark("P2b")
        E.wait("tensor", "s2v")
        vsrc = []
        for hk in range(NKV):
            for kc in range(16):
                poff = (hk % 2) * 64
                ch = hk // 2
                vsrc.append(vfm[poff:poff + 64, ch * T + kc * 128: ch * T + kc * 128 + 128])
        vT3 = vT.rearrange("p (s c) -> p s c", c=65)
        transpose_tiles4(
            vsrc,
            lambda b, pb, vT3=vT3: E.eng.tensor_copy(
                vT3[:, b * 4:b * 4 + 4, 0:64],
                pb[:, 0:256].rearrange("p (s c) -> p s c", c=64)),
            rows=64)
        E.drain("tensor", sem="t2g")
        E.drain("vector", sem="v2t")
        if tap == "vt":
            E.drain("vector", sem="v2q")
            E.wait("sync", "v2q")
            E.do("sync", lambda e: e.dma_start(out=dbg[:, 0:NKV * 16 * 65], in_=vT[:, :]), sem="qO0", n=16)
            return

        E.mark("PW")
        pw_abs = probs[:, 0:2048]
        pw_slots2 = ([attn[:, j * 1024:(j + 1) * 1024] for j in range(4)]
                     + [sfm[:, j * 1024:(j + 1) * 1024] for j in range(4)]
                     + [qfm[:, j * 1024:(j + 1) * 1024] for j in range(4)]
                     + [ftmp[:, 0:1024], hqfm[:, 10240:11264]])
        RING = [hqfm[:, i * 2048:(i + 1) * 2048] for i in (2, 3, 4, 0, 1)]
        wstage_ab = (ffnw[:, 0:4096], ffnw[:, 4096:8192])
        fl_sem = ("qGW", "qGW1")
        fl_cnt = {0: 0, 1: 0}

        def pw_items(subs, width):
            items = []
            for i, b in enumerate(subs):
                ab = pw_abs[:, i * 1024:i * 1024 + width]
                scr = [pw_slots2[i * 7 + j][:, 0:width] for j in range(7)]
                items.append((b, ab, scr))
            return items

        units = []
        for w_src, w_dst in ((wg, wgq_s), (wu, wuq_s)):
            for hb, (h0, hw) in enumerate(HBS):
                nhcl = hw // 128
                for pp in range(nhcl // 2):
                    units.append((w_src, w_dst, hb, h0, hw, 2 * pp))
        NU = len(units)
        qg_cnt = [0] * NU
        qg1_cnt = [0] * NU

        SEM4 = ("qG", "qG1", "qP0", "qP1")

        def pw_load(k):
            w_src = units[k][0]
            h0, hw, hcl0 = units[k][3], units[k][4], units[k][5]
            buf = RING[k % 5]
            for ci in range(2):
                hc = h0 // 128 + hcl0 + ci
                w_in = buf[:, ci * 1024:ci * 1024 + 1024]
                E.do("gpsimd", lambda e, w_in=w_in, hc=hc, w_src=w_src: e.dma_start(
                    out=w_in, in_=w_src[hc * 128:(hc + 1) * 128, :]),
                    sem=SEM4[k % 4], n=16)
            qg_cnt[k] = E.c[SEM4[k % 4]]

        for k in range(3):
            pw_load(k)
        E.wait("gpsimd", "t2g")
        pw_load(3)
        wsa = 0
        for k in range(NU):
            w_src, w_dst, hb, h0, hw, hcl0 = units[k]
            buf = RING[k % 5]
            E.wait("scalar", SEM4[k % 4], qg_cnt[k])
            E.wait("vector", SEM4[k % 4], qg_cnt[k])
            quant_group(pw_items([buf[:, 0:1024], buf[:, 1024:2048]], 1024))
            E.wait("tensor", "v2s")
            wst = wstage_ab[wsa]
            if hcl0 == 0:
                E.wait("vector", fl_sem[wsa])
            srcs = []
            for ci in range(2):
                for kc in range(NKC):
                    srcs.append(buf[:, ci * 1024 + kc * 128: ci * 1024 + (kc + 1) * 128])
            wst3 = wst[:, 0:NKC * hw].rearrange("p (k c) -> p k c", c=hw)

            def pw_evac(b, pb, wst3=wst3, hcl0=hcl0):
                ci = b // 2
                kc0 = (b % 2) * 4
                hcl = hcl0 + ci
                return E.eng.tensor_copy(
                    wst3[:, kc0:kc0 + 4, hcl * 128:hcl * 128 + 128],
                    pb[:, 0:512].rearrange("p (k c) -> p k c", c=128))
            transpose_tiles4(srcs, pw_evac, extra_last=(("v2g", 1),))
            E.drain("tensor", sem="t2g")
            if k + 4 < NU:
                E.wait("gpsimd", "t2g")
                pw_load(k + 4)
            if hcl0 + 2 == hw // 128:
                E.wait("gpsimd", "v2g")
                src = wst[:, 0:NKC * hw].rearrange("p (k c) -> p k c", c=hw)
                dst = w_dst[hb, :, :, 0:hw].rearrange("k p c -> p k c")
                E.do("gpsimd", lambda e, dst=dst, src=src: e.dma_start(
                    out=dst, in_=src), sem=fl_sem[wsa], n=16)
                fl_cnt[wsa] = E.c[fl_sem[wsa]]
                wsa ^= 1
        wd_stag = (hqfm[:, 0:2816], hqfm[:, 4096:6912])
        wd_qg = [0] * NKC
        wd_qg1 = [0] * NKC

        def wd_load(oc):
            stag = wd_stag[oc % 2]
            sm = "qG" if oc % 2 == 0 else "qG1"
            E.do("gpsimd", lambda e, oc=oc, stag=stag: e.dma_start(
                out=stag[:, 0:1408], in_=wd[oc * 128:(oc + 1) * 128, 0:1408]),
                sem=sm, n=16)
            E.do("gpsimd", lambda e, oc=oc, stag=stag: e.dma_start(
                out=stag[:, 1408:2816], in_=wd[oc * 128:(oc + 1) * 128, 1408:2816]),
                sem=sm, n=16)
            wd_qg[oc] = E.c[sm]

        E.wait("gpsimd", "t2g")
        wd_load(0)
        wd_load(1)
        for oc in range(NKC):
            stag = wd_stag[oc % 2]
            sm = "qG" if oc % 2 == 0 else "qG1"
            E.wait("scalar", sm, wd_qg[oc])
            E.wait("vector", sm, wd_qg[oc])
            for grp in range(2):
                subs = [stag[:, (grp * 2 + i) * 704:(grp * 2 + i + 1) * 704] for i in range(2)]
                quant_group(pw_items(subs, 704))
            E.wait("tensor", "v2s")
            wst = wstage_ab[wsa]
            E.wait("vector", fl_sem[wsa])
            transpose_tiles4(
                [stag[:, hc * 128:(hc + 1) * 128] for hc in range(NHC)],
                lambda b, pb, wst=wst: E.eng.tensor_copy(
                    wst[:, b * 512:min((b + 1) * 512, NHC * 128)],
                    pb[:, 0:min(512, NHC * 128 - b * 512)]),
                extra_last=(("v2g", 1),))
            E.drain("tensor", sem="t2g")
            if oc + 2 < NKC:
                E.wait("gpsimd", "t2g")
                wd_load(oc + 2)
            E.wait("gpsimd", "v2g")
            src = wst[:, 0:NHC * 128].rearrange("p (k c) -> p k c", c=128)
            dst = wdq_s[oc, :, :, :].rearrange("k p c -> p k c")
            E.do("gpsimd", lambda e, dst=dst, src=src: e.dma_start(
                out=dst, in_=src), sem=fl_sem[wsa], n=16)
            fl_cnt[wsa] = E.c[fl_sem[wsa]]
            wsa ^= 1
        if tap == "wprep":
            E.wait("sync", "qGW")
            E.wait("sync", "qGW1")
            E.drain("vector", sem="v2q")
            E.wait("sync", "v2q")
            E.do("sync", lambda e: e.dma_start(out=dbg[:, 0:2816], in_=wd_stag[1]), sem="qO0", n=16)
            return

        sc_free = {2: ("s2t", 0), 3: ("s2t", 0), 6: ("s2t", 0), 7: ("s2t", 0)}
        av4_free = [("s2t", 0)]
        av5_free = [("s2t", 0)]
        SC_ROT = (2, 3, 6, 7)
        for qc in range(NQC):
            E.mark(f"A{qc}")
            for ttl in range(4):
                tt = qc * 4 + ttl
                xa, qs = load_x(tt, tt % 2)
                E.wait("scalar", qs)
                E.wait("scalar", "t2s")
                E.wait("scalar", "qO0" if ttl % 2 == 0 else "qO1")
                xd = otmp[:, (ttl % 2) * 1024:(ttl % 2) * 1024 + 1024]
                xn_tile(tt, xa, xd)
                E.drain("scalar", sem="s2t")
                E.wait("tensor", "s2t")
                if ttl == 0:
                    E.wait("tensor", "t2v")
                xnfm3 = xnfm.rearrange("p (k c) -> p k c", c=512)
                transpose_tiles4(
                    [xd[:, kc * 128:(kc + 1) * 128] for kc in range(NKC)],
                    lambda b, pb, ttl=ttl, xnfm3=xnfm3: E.eng.tensor_copy(
                        xnfm3[:, b * 4:b * 4 + 4, ttl * 128:ttl * 128 + 128],
                        pb[:, 0:512].rearrange("p (k c) -> p k c", c=128)))
                E.drain("tensor", sem="t2s")
            E.drain("vector", sem="v2t")
            E.mark(f"B{qc}")
            if qc == 0:
                E.wait("sync", "qGW")
                E.wait("sync", "qGW1")
                E.wait("scalar", "v2s")
            SLQ = ("qW0", "qW1", "qO0", "qO1")
            ev_cnt = [E.c["s2v"]] * 4
            ld_cnt = [0] * NKC

            def wq_load(i, W):
                wb = wbuf[:, (i % 4) * 1024:(i % 4) * 1024 + 1024]
                E.wait("sync", "s2v", ev_cnt[i % 4])
                E.do("sync", lambda e, i=i, wb=wb, W=W: e.dma_start(
                    out=wb.rearrange("p (k o) -> p k o", o=128),
                    in_=W[:, i * 128:(i + 1) * 128].rearrange("(k p) o -> p k o", p=128)),
                    sem=SLQ[i % 4], n=16)
                ld_cnt[i] = E.c[SLQ[i % 4]]
            wq_load(0, Wq)
            wq_load(1, Wq)
            for hc2 in range(NKC):
                if hc2 + 2 < NKC:
                    wq_load(hc2 + 2, Wq)
                wb = wbuf[:, (hc2 % 4) * 1024:(hc2 % 4) * 1024 + 1024]
                pb = ps[2 + hc2 % 2]
                E.wait("tensor", SLQ[hc2 % 4], ld_cnt[hc2])
                E.wait("tensor", "v2t")
                E.wait("tensor", "s2v")
                for kc in range(NKC):
                    last = kc == NKC - 1
                    E.do("tensor", lambda e, pb=pb, wb=wb, kc=kc: e.matmul(
                        pb[:, 0:512], wb[:, kc * 128:(kc + 1) * 128],
                        xnfm[:, kc * 512:(kc + 1) * 512],
                        start=(kc == 0), stop=last),
                        sem=("t2s" if last else None))
                E.wait("scalar", "t2s")
                E.do("scalar", lambda e, pb=pb, hc2=hc2: e.activation(
                    qfm[:, hc2 * 512:(hc2 + 1) * 512], pb[:, 0:512], AF.Identity,
                    scale=cst[:, 2:3], bias=biasf[:, hc2:hc2 + 1]), sem="s2v")
                ev_cnt[hc2 % 4] = E.c["s2v"]
            E.drain("scalar", sem="s2t")
            if tap == "q" and qc == 0:
                E.wait("sync", "s2t")
                E.do("sync", lambda e: e.dma_start(out=dbg[:, 0:4096], in_=qfm[:, :]), sem="qO0", n=16)
                return
            E.mark(f"C{qc}")
            E.wait("tensor", "s2v")
            av_sbA = ftmp[:, 0:512]
            av_sbB = ftmp[:, 512:1024]
            probsB = hqfm[:, 4096:6144]
            SCA = (2, 6)
            SCB = (3, 7)
            for hkp in (0, 2):
                chk = hkp // 2
                for g in range(4):
                    hA = g * NKV + hkp
                    hB = hA + 1
                    chq = hA // 2
                    for ch in range(qc + 1):
                        diag = ch == qc
                        for j in range(4):
                            kc = ch * 4 + j
                            bA = SCA[j % 2]
                            bB = SCB[j % 2]
                            fsem, fval = sc_free[bA]
                            E.wait("tensor", fsem, fval)
                            E.do("tensor", lambda e, bA=bA, kc=kc, chk=chk, chq=chq: e.matmul(
                                ps[bA][:, 0:512],
                                kfm[0:64, chk * T + kc * 128: chk * T + kc * 128 + 128],
                                qfm[0:64, chq * 512:(chq + 1) * 512],
                                start=True, stop=True, tile_position=(0, 0)), sem="t2s")
                            fsem, fval = sc_free[bB]
                            E.wait("tensor", fsem, fval)
                            E.do("tensor", lambda e, bB=bB, kc=kc, chk=chk, chq=chq: e.matmul(
                                ps[bB][:, 0:512],
                                kfm[64:128, chk * T + kc * 128: chk * T + kc * 128 + 128],
                                qfm[64:128, chq * 512:(chq + 1) * 512],
                                start=True, stop=True, tile_position=(64, 0)), sem="t2s")
                            E.wait("scalar", "t2s")
                            if diag:
                                base = qc * 512 - kc * 128
                                E.do("scalar", lambda e, bA=bA, j=j: e.activation(
                                    probs[:, j * 512:(j + 1) * 512], ps[bA][:, 0:512], AF.Exp,
                                    bias=cst[:, 1:2]), sem="s2g")
                                sc_free[bA] = ("s2g", E.c["s2g"])
                                E.wait("gpsimd", "s2g")
                                E.do("gpsimd", lambda e, j=j, base=base: e.affine_select(
                                    probs[:, j * 512:(j + 1) * 512], probs[:, j * 512:(j + 1) * 512],
                                    pattern=[[1, 512]], compare_op=ALU.is_ge, fill=0.0,
                                    base=base, channel_multiplier=-1), sem="g2t")
                                E.do("scalar", lambda e, bB=bB, j=j: e.activation(
                                    probsB[:, j * 512:(j + 1) * 512], ps[bB][:, 0:512], AF.Exp,
                                    bias=cst[:, 1:2]), sem="s2g")
                                sc_free[bB] = ("s2g", E.c["s2g"])
                                E.wait("gpsimd", "s2g")
                                E.do("gpsimd", lambda e, j=j, base=base: e.affine_select(
                                    probsB[:, j * 512:(j + 1) * 512], probsB[:, j * 512:(j + 1) * 512],
                                    pattern=[[1, 512]], compare_op=ALU.is_ge, fill=0.0,
                                    base=base, channel_multiplier=-1), sem="g2t")
                            else:
                                E.do("scalar", lambda e, bA=bA, j=j: e.activation(
                                    probs[:, j * 512:(j + 1) * 512], ps[bA][:, 0:512], AF.Exp,
                                    bias=cst[:, 1:2]), sem="s2t")
                                sc_free[bA] = ("s2t", E.c["s2t"])
                                E.do("scalar", lambda e, bB=bB, j=j: e.activation(
                                    probsB[:, j * 512:(j + 1) * 512], ps[bB][:, 0:512], AF.Exp,
                                    bias=cst[:, 1:2]), sem="s2t")
                                sc_free[bB] = ("s2t", E.c["s2t"])
                        if diag:
                            E.wait("tensor", "g2t")
                        else:
                            E.wait("tensor", "s2t")
                        if ch == 0:
                            E.wait("tensor", av4_free[0][0], av4_free[0][1])
                            E.wait("tensor", av5_free[0][0], av5_free[0][1])
                        for j in range(4):
                            kc = ch * 4 + j
                            slot = (hkp * 16 + kc) * 65
                            E.do("tensor", lambda e, j=j, slot=slot, ch=ch: e.matmul(
                                ps[4][0:65, 0:512],
                                vT[:, slot:slot + 65],
                                probs[:, j * 512:(j + 1) * 512],
                                start=(ch == 0 and j == 0), stop=(diag and j == 3)))
                        for j in range(4):
                            kc = ch * 4 + j
                            slot = ((hkp + 1) * 16 + kc) * 65
                            E.do("tensor", lambda e, j=j, slot=slot, ch=ch, diag=diag: e.matmul(
                                ps[5][0:65, 0:512],
                                vT[:, slot:slot + 65],
                                probsB[:, j * 512:(j + 1) * 512],
                                start=(ch == 0 and j == 0), stop=(diag and j == 3)),
                                sem=("t2v" if (diag and j == 3) else None))
                    E.wait("vector", "t2v")
                    E.do("vector", lambda e: e.tensor_copy(av_sbA[0:65, 0:512], ps[4][0:65, 0:512]),
                         sem="v2t")
                    av4_free[0] = ("v2t", E.c["v2t"])
                    copyA = E.c["v2t"]
                    E.do("vector", lambda e: e.tensor_copy(av_sbB[0:65, 0:512], ps[5][0:65, 0:512]),
                         sem="v2t")
                    av5_free[0] = ("v2t", E.c["v2t"])
                    copyB = E.c["v2t"]
                    for av_sb, bnk, cpc, h, scol in ((av_sbA, 0, copyA, hA, 64),
                                                     (av_sbB, 1, copyB, hB, 68)):
                        E.wait("tensor", "v2t", max(tbank_evac[bnk], cpc))
                        for qt in range(4):
                            E.do("tensor", lambda e, bnk=bnk, qt=qt, av_sb=av_sb: e.transpose(
                                ps[bnk][:, qt * 65:qt * 65 + 65],
                                av_sb[0:65, qt * 128:qt * 128 + 128],
                                ident[0:65, 0:65]),
                                sem=("t2v" if qt == 3 else None))
                        E.wait("vector", "t2v")
                        for qt in range(4):
                            E.do("vector", lambda e, bnk=bnk, qt=qt, scol=scol: e.reciprocal(
                                stats[:, scol + qt:scol + qt + 1],
                                ps[bnk][:, qt * 65 + 64:qt * 65 + 65]))
                        E.drain("vector")
                        for qt in range(4):
                            E.do("vector", lambda e, bnk=bnk, qt=qt, h=h, scol=scol: e.tensor_scalar(
                                attn[:, qt * 1024 + h * 64: qt * 1024 + h * 64 + 64],
                                ps[bnk][:, qt * 65:qt * 65 + 64],
                                stats[:, scol + qt:scol + qt + 1], None, ALU.mult),
                                sem="v2t")
                        tbank_evac[bnk] = E.c["v2t"]
            E.drain("vector", sem="v2s", also=(("v2t", 1),))
            if tap == "attn" and qc == 0:
                E.drain("vector", sem="v2q")
                E.wait("sync", "v2q")
                E.do("sync", lambda e: e.dma_start(out=dbg[:, 0:4096], in_=attn[:, :]), sem="qO0", n=16)
                return
            E.mark(f"D{qc}")
            for qt in range(4):
                E.do("vector", lambda e, qt=qt: e.reduce_sum(
                    stats[:, 64 + qt:65 + qt], attn[:, qt * 1024:(qt + 1) * 1024], axis=AX.X))
                E.wait("scalar", "v2s")
                E.do("scalar", lambda e, qt=qt: e.activation(
                    otmp[:, 0:1024], attn[:, qt * 1024:(qt + 1) * 1024], AF.Square,
                    accum_out=stats[:, 68 + qt:69 + qt]))
                E.drain("scalar")
            E.drain("scalar", sem="s2v")
            E.drain("vector")
            E.wait("vector", "s2v")
            E.do("vector", lambda e: e.tensor_single_scalar(stats[:, 72:76], stats[:, 64:68], 1.0 / 1024.0, ALU.mult))
            E.drain("vector")
            E.do("vector", lambda e: e.tensor_single_scalar(stats[:, 76:80], stats[:, 68:72], 1.0 / 1024.0, ALU.mult))
            E.drain("vector")
            E.do("vector", lambda e: e.tensor_tensor(stats[:, 80:84], stats[:, 72:76], stats[:, 72:76], ALU.mult))
            E.drain("vector")
            E.do("vector", lambda e: e.tensor_tensor(stats[:, 84:88], stats[:, 76:80], stats[:, 80:84], ALU.subtract))
            E.drain("vector", sem="v2s")
            E.wait("scalar", "v2s")
            E.do("scalar", lambda e: e.activation(stats[:, 88:92], stats[:, 84:88], AF.Sqrt, bias=cst[:, 0:1]))
            E.drain("scalar", sem="s2v")
            E.wait("vector", "s2v")
            E.do("vector", lambda e: e.reciprocal(stats[:, 92:96], stats[:, 88:92]))
            E.drain("vector")
            E.do("vector", lambda e: e.scalar_tensor_tensor(stats[:, 72:76], stats[:, 72:76], -1.0, stats[:, 92:96], ALU.mult, ALU.mult))
            E.drain("vector", sem="v2s")
            E.wait("scalar", "v2s")
            for qt in range(4):
                E.wait("scalar", "t2s")
                E.wait("scalar", "qO0" if qt % 2 == 0 else "qO1")
                xd = otmp[:, (qt % 2) * 1024:(qt % 2) * 1024 + 1024]
                E.do("scalar", lambda e, qt=qt, xd=xd: e.activation(
                    xd, attn[:, qt * 1024:(qt + 1) * 1024], AF.Identity,
                    scale=stats[:, 92 + qt:93 + qt], bias=stats[:, 72 + qt:73 + qt]))
                E.drain("scalar", sem="s2t")
                E.wait("tensor", "s2t")
                if qt == 0:
                    E.wait("tensor", "t2g")
                xnfm3 = xnfm.rearrange("p (k c) -> p k c", c=512)
                transpose_tiles4(
                    [xd[:, kc * 128:(kc + 1) * 128] for kc in range(NKC)],
                    lambda b, pb, qt=qt, xnfm3=xnfm3: E.eng.tensor_copy(
                        xnfm3[:, b * 4:b * 4 + 4, qt * 128:qt * 128 + 128],
                        pb[:, 0:512].rearrange("p (k c) -> p k c", c=128)))
                E.drain("tensor", sem="t2s")
            E.drain("vector")
            for kc in range(NKC):
                E.do("vector", lambda e, kc=kc: e.tensor_scalar(
                    xnfm[:, kc * 512:(kc + 1) * 512], xnfm[:, kc * 512:(kc + 1) * 512],
                    biasf[:, 20 + kc:21 + kc], biasf[:, 28 + kc:29 + kc],
                    ALU.mult, ALU.add), sem="v2t")
            E.drain("vector", sem="v2t")
            E.wait("vector", "t2s")
            xq_slots = ([hqfm[:, j * 512:(j + 1) * 512] for j in range(22)]
                        + [qfm[:, j * 512:(j + 1) * 512] for j in range(6)])
            for pp in range(2):
                xts = []
                for ci in range(2):
                    ttl = 2 * pp + ci
                    tt = qc * 4 + ttl
                    xa, qs = load_x(tt, tt % 2)
                    E.wait("scalar", qs)
                    xd = attn[:, ttl * 1024:(ttl + 1) * 1024]
                    xn_tile(tt, xa, xd)
                    xts.append(xd)
                E.drain("scalar")
                E.wait("scalar", "t2s")
                items = []
                si = 0
                for i_t, xd in enumerate(xts):
                    for half in range(2):
                        bi = xd[:, half * 512:half * 512 + 512]
                        ab = otmp[:, (2 * i_t + half) * 512:(2 * i_t + half + 1) * 512]
                        scr = [xq_slots[si + j] for j in range(7)]
                        si += 7
                        items.append((bi, ab, scr))
                quant_group(items)
            xq_done = E.c["v2s"]
            ev_cnt = [E.c["s2v"]] * 4
            ld_cnt = [0] * NKC
            wq_load(0, Wo)
            wq_load(1, Wo)
            for oc in range(NKC):
                if oc + 2 < NKC:
                    wq_load(oc + 2, Wo)
                wb = wbuf[:, (oc % 4) * 1024:(oc % 4) * 1024 + 1024]
                pb = ps[2 + oc % 2]
                E.wait("tensor", SLQ[oc % 4], ld_cnt[oc])
                E.wait("tensor", "v2t")
                E.wait("tensor", "s2v")
                for kc in range(NKC):
                    last = kc == NKC - 1
                    E.do("tensor", lambda e, pb=pb, wb=wb, kc=kc: e.matmul(
                        pb[:, 0:512], wb[:, kc * 128:(kc + 1) * 128],
                        xnfm[:, kc * 512:(kc + 1) * 512],
                        start=(kc == 0), stop=last),
                        sem=("t2s" if last else None))
                E.wait("scalar", "t2s")
                E.do("scalar", lambda e, pb=pb, oc=oc: e.activation(
                    sfm[:, oc * 512:(oc + 1) * 512], pb[:, 0:512], AF.Identity,
                    bias=biasf[:, 8 + oc:9 + oc]), sem="s2v")
                ev_cnt[oc % 4] = E.c["s2v"]
            E.drain("scalar", sem="s2t", also=(("s2v", 1),))
            if tap == "s" and qc == 0:
                E.wait("sync", "s2v")
                E.do("sync", lambda e: e.dma_start(out=dbg[:, 0:4096], in_=sfm[:, :]), sem="qO0", n=16)
                return
            E.wait("tensor", "v2s", xq_done)
            for ttl in range(4):
                xd = attn[:, ttl * 1024:(ttl + 1) * 1024]
                xnfm3 = xnfm.rearrange("p (k c) -> p k c", c=512)
                transpose_tiles4(
                    [xd[:, kc * 128:(kc + 1) * 128] for kc in range(NKC)],
                    lambda b, pb, ttl=ttl, xnfm3=xnfm3: E.eng.tensor_copy(
                        xnfm3[:, b * 4:b * 4 + 4, ttl * 128:ttl * 128 + 128],
                        pb[:, 0:512].rearrange("p (k c) -> p k c", c=128)))
            E.drain("tensor", sem="t2s")
            E.drain("vector", sem="v2t")
            if tap == "xq" and qc == 0:
                E.drain("vector", sem="v2q")
                E.wait("sync", "v2q")
                E.do("sync", lambda e: e.dma_start(out=dbg[:, 0:4096], in_=xnfm[:, :]), sem="qO0", n=16)
                return
            E.mark(f"E{qc}")
            E.wait("gpsimd", "qGW")
            E.wait("gpsimd", "qGW1")
            pg_free = [E.c["s2v"], E.c["s2v"]]
            pu_free = [E.c["v2t"], E.c["v2t"]]

            def h_quant(hbq):
                h0q, hwq = HBS[hbq]
                h2 = hwq // 2
                colq = attn if hbq % 2 == 0 else probs
                nsub_scr = ([qfm[:, j * 256:(j + 1) * 256] for j in range(16)]
                            + [otmp[:, j * 256:(j + 1) * 256] for j in range(8)]
                            + [attn[:, 3072 + j * 256:3072 + (j + 1) * 256] for j in range(4)])
                for grp in range(2):
                    items = []
                    for t in range(2):
                        qt = grp * 2 + t
                        for half in range(2):
                            bi = colq[:, qt * hwq + half * h2: qt * hwq + half * h2 + h2]
                            ab = attn[:, 2048 + (2 * t + half) * 256: 2048 + (2 * t + half) * 256 + h2]
                            scr = [nsub_scr[(2 * t + half) * 7 + j][:, 0:h2] for j in range(7)]
                            items.append((bi, ab, scr))
                    quant_group(items)

            def h_transpose(hbq):
                h0q, hwq = HBS[hbq]
                colq = attn if hbq % 2 == 0 else probs
                E.drain("vector", sem="v2t")
                E.wait("tensor", "v2t")
                srcs = []
                for qt in range(4):
                    for i in range(hwq // 128):
                        srcs.append(colq[:, qt * hwq + i * 128: qt * hwq + i * 128 + 128])
                nbq = hwq // 128
                hq3 = hqfm.rearrange("p (k c) -> p k c", c=512)

                def hq_evac(b, pb, hq3=hq3, h0q=h0q, nbq=nbq):
                    return E.eng.tensor_copy(
                        hq3[:, h0q // 128:h0q // 128 + nbq, b * 128:b * 128 + 128],
                        pb[:, 0:nbq * 128].rearrange("p (k c) -> p k c", c=128))
                transpose_tiles4(srcs, hq_evac, nb=nbq)
                E.drain("tensor", sem="t2s")

            for hb, (h0, hw) in enumerate(HBS):
                E.wait("gpsimd", "s2v", max(pg_free))
                E.wait("gpsimd", "v2t", max(pu_free))
                E.do("gpsimd", lambda e, hb=hb, hw=hw: e.dma_start(
                    out=ffnw[:, 0:NKC * hw].rearrange("p (k c) -> p k c", c=hw),
                    in_=wgq_s[hb, :, :, 0:hw].rearrange("k p c -> p k c")), sem="qG", n=16)
                E.do("gpsimd", lambda e, hb=hb, hw=hw: e.dma_start(
                    out=ffnw[:, 4096:4096 + NKC * hw].rearrange("p (k c) -> p k c", c=hw),
                    in_=wuq_s[hb, :, :, 0:hw].rearrange("k p c -> p k c")), sem="qG1", n=16)
                if hb > 0:
                    h_quant(hb - 1)
                colb = attn if hb % 2 == 0 else probs
                for qt in range(4):
                    pg = ps[2 + 2 * (qt % 2)]
                    pu = ps[3 + 2 * (qt % 2)]
                    par = qt % 2
                    E.wait("tensor", "qG")
                    E.wait("tensor", "qG1")
                    E.wait("tensor", "s2v", pg_free[par])
                    E.wait("tensor", "v2t", pu_free[par])
                    for kc in range(NKC):
                        last = kc == NKC - 1
                        E.do("tensor", lambda e, pg=pg, kc=kc, qt=qt, hw=hw: e.matmul(
                            pg[:, 0:hw], xnfm[:, kc * 512 + qt * 128: kc * 512 + qt * 128 + 128],
                            ffnw[:, kc * hw:(kc + 1) * hw],
                            start=(kc == 0), stop=last),
                            sem=("t2s" if last else None))
                    for kc in range(NKC):
                        last = kc == NKC - 1
                        E.do("tensor", lambda e, pu=pu, kc=kc, qt=qt, hw=hw: e.matmul(
                            pu[:, 0:hw], xnfm[:, kc * 512 + qt * 128: kc * 512 + qt * 128 + 128],
                            ffnw[:, 4096 + kc * hw:4096 + (kc + 1) * hw],
                            start=(kc == 0), stop=last),
                            sem=("t2v" if last else None))
                    E.wait("scalar", "t2s")
                    E.wait("scalar", "v2t", pu_free[par])
                    sg = ftmp[:, par * 512:par * 512 + hw]
                    E.do("scalar", lambda e, pg=pg, sg=sg, hw=hw: e.activation(
                        sg, pg[:, 0:hw], AF.Silu))
                    E.drain("scalar", sem="s2v")
                    pg_free[par] = E.c["s2v"]
                    E.wait("vector", "t2v")
                    E.wait("vector", "s2v")
                    E.do("vector", lambda e, pu=pu, sg=sg, qt=qt, hw=hw, colb=colb: e.tensor_tensor(
                        colb[:, qt * hw:(qt + 1) * hw], sg, pu[:, 0:hw], ALU.mult), sem="v2t")
                    pu_free[par] = E.c["v2t"]
                if hb > 0:
                    h_transpose(hb - 1)
            h_quant(len(HBS) - 1)
            h_transpose(len(HBS) - 1)
            E.drain("vector", sem="v2t")
            sg_free = [E.c["v2s"]]
            hq_done = E.c["v2t"]
            dn_free = [max(pu_free), max(pu_free)]
            dn_silu = [max(pg_free), max(pg_free)]
            for oc in range(NKC):
                E.wait("gpsimd", "v2t", max(max(pu_free), hq_done))
                E.wait("gpsimd", "s2v", max(dn_silu))
                E.do("gpsimd", lambda e, oc=oc: e.dma_start(
                    out=wdbuf.rearrange("p (k c) -> p k c", c=128),
                    in_=wdq_s[oc, :, :, :].rearrange("k p c -> p k c")), sem="qG", n=16)
                pd = ps[2 + oc % 2]
                E.wait("tensor", "qG")
                E.wait("tensor", "qG1")
                E.wait("tensor", "v2t", max(max(pu_free), hq_done))
                E.wait("tensor", "s2v", dn_silu[oc % 2])
                for hc in range(NHC):
                    last = hc == NHC - 1
                    E.do("tensor", lambda e, pd=pd, hc=hc: e.matmul(
                        pd[:, 0:512], wdbuf[:, hc * 128:(hc + 1) * 128],
                        hqfm[:, hc * 512:(hc + 1) * 512],
                        start=(hc == 0), stop=last),
                        sem=("t2s" if last else None))
                E.wait("scalar", "t2s")
                E.wait("scalar", "v2s", sg_free[0])
                sgb = ftmp[:, (oc % 2) * 512:(oc % 2) * 512 + 512]
                E.do("scalar", lambda e, pd=pd, sgb=sgb: e.activation(
                    sgb, pd[:, 0:512], AF.Silu))
                E.drain("scalar", sem="s2v")
                dn_silu[oc % 2] = E.c["s2v"]
                E.wait("vector", "s2v")
                E.do("vector", lambda e, oc=oc, sgb=sgb: e.tensor_tensor(
                    sfm[:, oc * 512:(oc + 1) * 512], sgb,
                    sfm[:, oc * 512:(oc + 1) * 512], ALU.mult), sem="v2t", also=(("v2s", 1),))
                dn_free[oc % 2] = E.c["v2t"]
                sg_free[0] = E.c["v2s"]
            sc_free[2] = ("s2v", E.c["s2v"])
            sc_free[3] = ("s2v", E.c["s2v"])
            av4_free[0] = ("s2v", E.c["s2v"])
            av5_free[0] = ("v2t", E.c["v2t"])
            E.drain("vector", sem="v2t")
            if tap == "sg" and qc == 0:
                E.drain("vector", sem="v2q")
                E.wait("sync", "v2q")
                E.do("sync", lambda e: e.dma_start(out=dbg[:, 0:4096], in_=sfm[:, :]), sem="qO0", n=16)
                return
            E.mark(f"F{qc}")
            for ttl in range(4):
                tt = qc * 4 + ttl
                xa, qs = load_x(tt, tt % 2)
                ob = otmp[:, (ttl % 2) * 1024:(ttl % 2) * 1024 + 1024]
                qo = "qO0" if ttl % 2 == 0 else "qO1"
                E.wait("vector", qs)
                E.wait("vector", qo)
                E.wait("tensor", "v2t")
                transpose_tiles4(
                    [sfm[:, oc * 512 + ttl * 128: oc * 512 + ttl * 128 + 128] for oc in range(NKC)],
                    lambda b, pb, xa=xa, ob=ob: E.eng.tensor_tensor(
                        ob[:, b * 512:(b + 1) * 512], pb[:, 0:512],
                        xa[:, b * 512:(b + 1) * 512], ALU.add),
                    extra_last=(("v2q", 1),))
                E.wait("sync", "v2q")
                E.do("sync", lambda e, tt=tt, ob=ob: e.dma_start(
                    out=y[tt * 128:(tt + 1) * 128, :], in_=ob), sem=qo, n=16)
            E.drain("vector", sem="v2t")
        E.mark("END")

    if sim_check:
        logs = {}
        for en in ("sync", "tensor", "vector", "scalar", "gpsimd"):
            lg = []
            Es = EV(en, None, sems, log=lg)
            schedule(Es)
            for s_ in ("qC", "qX0", "qX1", "qW0", "qW1", "qO0", "qO1", "qG", "qGW"):
                Es.wait(en, s_)
            logs[en] = lg
        _check_deadlock(logs)

    with nc.Block() as block:
        def run(en, eng):
            Ei = EV(en, eng, sems)
            schedule(Ei)
            for s_ in ("qC", "qX0", "qX1", "qW0", "qW1", "qO0", "qO1", "qG", "qGW"):
                Ei.wait(en, s_)

        @block.sync
        def _(eng):
            run("sync", eng)

        @block.tensor
        def _(eng):
            run("tensor", eng)

        @block.vector
        def _(eng):
            run("vector", eng)

        @block.scalar
        def _(eng):
            run("scalar", eng)

        @block.gpsimd
        def _(eng):
            run("gpsimd", eng)

    cx.close()
    return nc


_NC_CACHE = {}


def kernel(L0_attn_x, L0_attn_Wq, L0_attn_bq, L0_attn_Wk, L0_attn_bk,
           L0_attn_Wv, L0_attn_bv, L0_attn_ln_g, L0_attn_ln_b, L0_attn_Wo,
           L0_attn_bo, L0_ffn_w_gate, L0_ffn_w_up, L0_ffn_w_down):
    xs = np.ascontiguousarray(np.asarray(L0_attn_x, dtype=np.float32))
    w = {
        "Wq": np.ascontiguousarray(np.asarray(L0_attn_Wq, dtype=np.float32)),
        "bq": np.ascontiguousarray(np.asarray(L0_attn_bq, dtype=np.float32)),
        "Wk": np.ascontiguousarray(np.asarray(L0_attn_Wk, dtype=np.float32)),
        "bk": np.ascontiguousarray(np.asarray(L0_attn_bk, dtype=np.float32)),
        "Wv": np.ascontiguousarray(np.asarray(L0_attn_Wv, dtype=np.float32)),
        "bv": np.ascontiguousarray(np.asarray(L0_attn_bv, dtype=np.float32)),
        "lng": np.ascontiguousarray(np.asarray(L0_attn_ln_g, dtype=np.float32)),
        "lnb": np.ascontiguousarray(np.asarray(L0_attn_ln_b, dtype=np.float32)),
        "Wo": np.ascontiguousarray(np.asarray(L0_attn_Wo, dtype=np.float32)),
        "bo": np.ascontiguousarray(np.asarray(L0_attn_bo, dtype=np.float32)),
        "wg": np.ascontiguousarray(np.asarray(L0_ffn_w_gate, dtype=np.float32)),
        "wu": np.ascontiguousarray(np.asarray(L0_ffn_w_up, dtype=np.float32)),
        "wd": np.ascontiguousarray(np.asarray(L0_ffn_w_down, dtype=np.float32)),
    }
    if "nc" not in _NC_CACHE:
        _NC_CACHE["nc"] = build_nc()
    nc = _NC_CACHE["nc"]
    in_maps = [dict(w, x=np.ascontiguousarray(xs[c])) for c in range(8)]
    res = run_bass_kernel_spmd(nc, in_maps, core_ids=list(range(8)))
    return np.stack([np.asarray(res.results[c]["y"], dtype=np.float32) for c in range(8)], axis=0)
